# version 05
# baseline (speedup 1.0000x reference)
import math
import numpy as np
import ml_dtypes
import concourse.bass as bass
import concourse.mybir as mybir
from concourse.bass_utils import run_bass_kernel_spmd

F32 = mybir.dt.float32
BF16 = mybir.dt.bfloat16
U32 = mybir.dt.uint32
AF = mybir.ActivationFunctionType
OP = mybir.AluOpType
AX = mybir.AxisListType
BF = ml_dtypes.bfloat16

DUMP = set()
DBG_RESULTS = []

T = 1024
CH = 512
HID = 4096
JT = 172
FT = 86
ISD = 1.0 / math.sqrt(128.0)
THRESH = 0.001
EPS = 1e-5
GP_CONVERT = False
PROJ_SERIAL = False


ESEM = {"tensor": "tP", "vector": "vP", "scalar": "aP", "gpsimd": "gP"}


class Prog:

    def __init__(self):
        self.ops = []
        self.cnt = {}
        self.handles = {}

    def v(self, name):
        return self.cnt.get(name, 0)

    def op(self, engine, fn, waits=(), inc=None, inc_by=1, barrier=False):
        waits = list(waits)
        if engine in ("vector", "scalar", "gpsimd") and not barrier:
            sem = ESEM[engine]
            prev = self.v(sem)
            if prev > 0:
                waits.append((sem, prev))
            if inc is None:
                inc, inc_by = sem, 1
        waits = [(s, v) for (s, v) in waits if v > 0]
        tgt = None
        if inc is not None:
            self.cnt[inc] = self.cnt.get(inc, 0) + inc_by
            tgt = (inc, inc_by)

        def emit(eng, _fn=fn, _w=waits, _t=tgt, _P=self):
            for s, vv in _w:
                eng.wait_ge(_P.handles[s], vv)
            inst = _fn(eng)
            if _t is not None:
                inst.then_inc(_P.handles[_t[0]], _t[1])
        self.ops.append((engine, emit))
        return self.v(inc) if inc else None


def build_program(dbg=()):
    nc = bass.Bass("TRN2", target_bir_lowering=False, debug=False,
                   dynamic_dma_scratch_size=8192)
    P = Prog()

    def inp(name, shape, dt):
        return nc.dram_tensor(name, list(shape), dt, kind="ExternalInput").ap()

    xT = inp("xT", (HID, T), F32)
    WTd = {p: inp(f"W{p}T", (HID, HID), F32) for p in "qkvo"}
    bTd = {p: inp(f"b{p}T", (128, 32), F32) for p in "qkvo"}
    qw_gu = inp("qw_gu", (64, 22016), U32)
    qz_gu = inp("qz_gu", (32, 2752), U32)
    sc_gu = inp("sc_gu", (32, 22016), F32)
    gidx_gu = inp("gidx_gu", (128, 32), F32)
    gidx_guw = inp("gidx_guw", (128, 4), F32)
    qw_dn = inp("qw_dn", (176, 4096), U32)
    qz_dn = inp("qz_dn", (96, 512), U32)
    sc_dn = inp("sc_dn", (96, 4096), F32)
    gidx_dn = inp("gidx_dn", (128, 86), F32)
    gidx_dnw = inp("gidx_dnw", (128, 11), F32)

    out = nc.dram_tensor("out", [T, HID], F32, kind="ExternalOutput").ap()

    mask_np = np.full((128, 128), -1e30, np.float32)
    for b in range(4):
        mask_np[b * 32:(b + 1) * 32, b * 32:(b + 1) * 32] = 0.0
    c_ident = nc.inline_tensor(np.eye(128, dtype=BF), "c_ident").ap()
    c_mask = nc.inline_tensor(mask_np, "c_mask").ap()
    c_iota32 = nc.inline_tensor(np.tile(np.arange(32, dtype=np.float32), (128, 1)), "c_iota32").ap()
    c_iota86 = nc.inline_tensor(np.tile(np.arange(86, dtype=np.float32), (128, 1)), "c_iota86").ap()
    c_onesrow = nc.inline_tensor(np.ones((1, 128), BF), "c_onesrow").ap()
    c_onescol = nc.inline_tensor(np.ones((128, 1), BF), "c_onescol").ap()
    c_shift = nc.inline_tensor((np.arange(128) % 8 * 4).astype(np.uint32).reshape(128, 1),
                               "c_shift").ap()
    c_eps = nc.inline_tensor(np.array([[EPS]], np.float32), "c_eps").ap()

    guW = nc.dram_tensor("guW_loc", [4, JT * 128 * 128], BF16, kind="Internal")
    guWall = nc.dram_tensor("guW_all", [32, JT * 128 * 128], BF16, kind="Internal", addr_space="Shared")
    dnW = nc.dram_tensor("dnW_loc", [11, 8 * 128 * 512], BF16, kind="Internal")
    dnWall = nc.dram_tensor("dnW_all", [88, 8 * 128 * 512], BF16, kind="Internal", addr_space="Shared")
    zg_d = nc.dram_tensor("zg_d", [32, 22016], BF16, kind="Internal")
    zd_d = nc.dram_tensor("zd_d", [96, 4096], BF16, kind="Internal")
    bar_i = nc.dram_tensor("bar_i", [1, 64], F32, kind="Internal")
    bar_o = nc.dram_tensor("bar_o", [1, 64], F32, kind="Internal")

    dbg_outs = {}
    for nm, shape, dt in dbg:
        dbg_outs[nm] = nc.dram_tensor(nm, list(shape), dt, kind="ExternalOutput").ap()

    from contextlib import ExitStack
    with ExitStack() as ctx:
        ARENA_B = 220928
        arena = ctx.enter_context(nc.sbuf_tensor("arena", [128, ARENA_B], mybir.dt.uint8))
        acast = {F32: arena.bitcast(F32), BF16: arena.bitcast(BF16), U32: arena.bitcast(U32)}

        _off = [0]

        def alloc(nbytes, align=512):
            o = (_off[0] + align - 1) // align * align
            _off[0] = o + nbytes
            return o

        def vap(off, dt, rows, dims):
            es = mybir.dt.size(dt)
            assert off % es == 0
            return bass.AP(acast[dt], off // es,
                           [[ARENA_B // es, rows]] + [list(d) for d in dims])

        def view(off, rows, cols, dt):
            return vap(off, dt, rows, [(1, cols)])

        o_X = alloc(65536)
        o_BIG = alloc(98304)
        o_ZONE = alloc(24576)
        o_pool = alloc(6144)
        o_rowz = alloc(6144)
        o_rowbf = alloc(4096)
        o_mrow = o_rowbf + 3072
        o_outb = alloc(4096)
        o_tmpf = alloc(2048)
        o_xg = alloc(1024)
        o_hg = alloc(1024)
        o_zg = alloc(1024)
        o_zd = alloc(2048)
        o_G = alloc(512)
        o_GT = alloc(512)
        o_ident = alloc(256, 64)
        o_mask = alloc(512, 64)
        o_iota32 = alloc(128, 64)
        o_iota86 = alloc(352, 64)
        o_onesrow = alloc(256, 64)
        o_onescol = alloc(256, 64)
        o_shiftv = alloc(512, 64)
        o_bT = {p: alloc(128, 64) for p in "qkvo"}
        o_gidxg = alloc(128, 64)
        o_gidxgw = alloc(64, 64)
        o_gidxd = alloc(352, 64)
        o_gidxdw = alloc(64, 64)
        o_eps = alloc(64, 64)
        assert _off[0] <= ARENA_B, _off[0]

        def Xt(ct, ch):
            return view(o_X + (ct * T + ch * CH) * 2, 128, CH, BF16)

        def Xcol(ct, c0, w):
            return view(o_X + (ct * T + c0) * 2, 128, w, BF16)

        s_ident = view(o_ident, 128, 128, BF16)
        s_mask = view(o_mask, 128, 128, F32)
        s_iota32 = view(o_iota32, 128, 32, F32)
        s_iota86 = view(o_iota86, 128, 86, F32)
        s_onesrow = view(o_onesrow, 1, 128, BF16)
        s_onescol = view(o_onescol, 128, 1, BF16)
        s_shift = view(o_shiftv, 128, 1, U32)
        s_bT = {p: view(o_bT[p], 128, 32, F32) for p in "qkvo"}
        s_gidxg = view(o_gidxg, 128, 32, F32)
        s_gidxgw = view(o_gidxgw, 128, 4, F32)
        s_gidxd = view(o_gidxd, 128, 86, F32)
        s_gidxdw = view(o_gidxdw, 128, 11, F32)
        s_tmpf = view(o_tmpf, 128, 512, F32)
        s_xg = view(o_xg, 32, 512, BF16)
        s_hg = view(o_hg, 86, 512, BF16)
        s_zg = [view(o_zg + i * 512, 32, 256, BF16) for i in range(2)]
        s_zd = [view(o_zd + i * 1024, 96, 512, BF16) for i in range(2)]
        s_G32 = view(o_G, 128, 32, BF16)
        s_G86 = view(o_G, 128, 86, BF16)
        s_GT = view(o_GT, 86, 128, BF16)
        s_pool = [view(o_pool + i * 2048, 128, 512, F32) for i in range(3)]
        s_row = [view(o_rowz + i * 2048, 1, 512, F32) for i in range(3)]
        s_rowbf = [view(o_rowbf + i * 1024, 1, 512, BF16) for i in range(3)]
        s_mrow = view(o_mrow, 1, 512, BF16)
        s_outb = [view(o_outb + i * 2048, 128, 512, F32) for i in range(2)]

        z = o_ZONE
        s_wf32 = [view(z + i * 2048, 128, 512, F32) for i in range(3)]
        s_wbf = [view(z + 6144 + i * 1024, 128, 512, BF16) for i in range(4)]
        o_stage = z + 10240
        s_sS = [view(z + 18432 + i * 512, 128, 128, F32) for i in range(2)]
        s_eS = [view(z + 19456 + i * 256, 128, 128, BF16) for i in range(2)]
        s_pnS = [view(z + 19968 + i * 256, 128, 128, BF16) for i in range(2)]
        s_pTS = [view(z + 20480 + i * 256, 128, 128, BF16) for i in range(2)]
        s_vbS = [view(z + 20992 + i * 256, 128, 128, BF16) for i in range(2)]
        s_rS = view(z + 21504, 128, 4, F32)
        s_guslab = [view(z + i * 8192, 128, 4096, BF16) for i in range(2)]
        s_dnslab = [view(z + 16384 + i * 1024, 128, 512, BF16) for i in range(4)]
        s_gtmp = [view(z + 20480 + i * 1024, 128, 512, BF16) for i in range(2)]
        s_sqt = [view(z + 22528 + i * 1024, 128, 512, BF16) for i in range(2)]

        wb = o_BIG
        w_qw = view(wb, 128, 2048, U32)
        w_nib = view(wb + 8192, 128, 2048, U32)
        w_bc = view(wb + 16384, 128, 2048, BF16)
        w_w1 = view(wb + 20480, 128, 2048, BF16)
        w_sc = view(wb + 24576, 96, 2048, F32)
        w_scb = view(wb + 32768, 96, 2048, BF16)
        w_znib_off = wb + 36864
        w_znib = view(w_znib_off, 96, 2048, U32)
        w_zf = view(wb + 45056, 96, 2048, F32)
        w_zb = view(wb + 53248, 96, 2048, BF16)
        w_zq = view(wb + 57344, 96, 256, U32)
        QTB = o_BIG // 2
        KTB = o_BIG // 2 + 16384
        VTB = o_BIG // 2 + 32768

        def Hbt(ft):
            return view(o_BIG + ft * 1024, 128, 512, BF16)

        bank = [ctx.enter_context(nc.psum_tensor(f"bank{i}", [128, 512], F32))
                for i in range(8)]

        def pb(i, rows=128, cols=512, col0=0):
            return bank[i][0:rows, col0:col0 + cols]

        def pbf(i, rows=128, cols=1024, col0=0):
            return bank[i].bitcast(BF16)[0:rows, col0:col0 + cols]

        for s in ["dS", "dG", "dK", "dD", "tP", "vP", "aP", "gP", "c1", "c2", "c3",
                  "w0", "w1", "w2", "dq", "ds1", "ds2", "z1s", "z2s", "z3s",
                  "ga", "gb", "za", "zb", "na", "nb", "nc2", "nd", "ea", "eb",
                  "oa", "ob"]:
            P.handles[s] = ctx.enter_context(nc.semaphore(s))

        def dma(dst, src, q="sync", waits=(), sem=None):
            if sem is None:
                sem = "dS" if q == "sync" else "dG"
            return P.op(q, lambda eng, d=dst, s=src: eng.dma_start(d, s),
                        waits=waits, inc=sem, inc_by=16)

        def mark(engine, fn, waits=()):
            sem = {"tensor": "tP", "vector": "vP", "scalar": "aP", "gpsimd": "gP"}[engine]
            return P.op(engine, fn, waits=waits, inc=sem)

        def quiet(engine, fn, waits=()):
            P.op(engine, fn, waits=waits)

        dbg_names = []

        def dump(name, src_ap, rows, cols, dt, waits=()):
            if name not in DUMP:
                return
            tt = nc.dram_tensor("dbg_" + name, [rows, cols], dt, kind="ExternalOutput")
            dbg_names.append(name)
            d = dma(bass.AP(tt, 0, [[cols, rows], [1, cols]]), src_ap,
                    waits=list(waits) + [("dD", P.v("dD"))], sem="dD")
            for e in ("tensor", "vector", "scalar", "gpsimd"):
                P.op(e, lambda eng: None, waits=[("dD", d)], barrier=True)
        build_program.dbg_names = dbg_names

        for dst, src in [(s_ident, c_ident), (s_mask, c_mask), (s_iota32, c_iota32),
                         (s_iota86, c_iota86),
                         (s_onescol, c_onescol), (s_shift, c_shift),
                         (s_gidxg, gidx_gu), (s_gidxgw, gidx_guw),
                         (s_gidxd, gidx_dn), (s_gidxdw, gidx_dnw)]:
            dma(dst, src, sem="dK")
        dma(s_onesrow, c_onesrow, sem="dK")
        s_eps = view(o_eps, 1, 1, F32)
        nc.const_aps.register(F32, EPS, s_eps)
        dma(s_eps, c_eps, sem="dK")
        for p in "qkvo":
            dma(s_bT[p], bTd[p], sem="dK")
        d_consts = ("dK", P.v("dK"))

        xsrc = bass.AP(xT.tensor, 0, [[T, 128], [128 * T, 32], [1, T]])
        xdst = vap(o_X, BF16, 128, [(T, 32), (1, T)])
        dma(xdst, xsrc, q="gpsimd")
        d_x = P.v("dG")

        v_z = 0
        d_zst = 0
        for rows, totw, qz_t, qz_w, sc_t, sc_w, zdst_t, zdst_w in [
                (32, 2752, qz_gu.tensor, 2752, sc_gu.tensor, 22016, zg_d, 22016),
                (96, 512, qz_dn.tensor, 512, sc_dn.tensor, 4096, zd_d, 4096)]:
            nchunk = (totw + 255) // 256
            for k in range(nchunk):
                w0 = k * 256
                nw = min(256, totw - w0)
                ncol = nw * 8
                dzq = dma(w_zq[0:rows, 0:nw],
                          bass.AP(qz_t, w0, [[qz_w, rows], [1, nw]]),
                          waits=[("vP", v_z)], sem="z1s")
                dsc = dma(w_sc[0:rows, 0:ncol],
                          bass.AP(sc_t, w0 * 8, [[sc_w, rows], [1, ncol]]),
                          waits=[("vP", v_z)], sem="z2s")
                for s in range(8):
                    ap_o = bass.AP(acast[U32], w_znib_off // 4 + s,
                                   [[ARENA_B // 4, rows], [8, nw]])
                    quiet("vector", lambda eng, d=ap_o, sh=4 * s, r=rows, n=nw: eng.tensor_scalar(
                        d, w_zq[0:r, 0:n], sh, 15,
                        OP.logical_shift_right, OP.bitwise_and),
                        waits=[("z1s", dzq)] if s == 0 else [])
                quiet("vector", lambda eng, r=rows, n=ncol: eng.tensor_scalar(
                    w_zf[0:r, 0:n], w_znib[0:r, 0:n], 1.0, -1.0, OP.add, OP.mult))
                v_z = mark("vector", lambda eng, r=rows, n=ncol: eng.tensor_tensor(
                    w_zb[0:r, 0:n], w_zf[0:r, 0:n], w_sc[0:r, 0:n], OP.mult),
                    waits=[("z2s", dsc), ("z3s", d_zst)])
                d_zst = dma(bass.AP(zdst_t, w0 * 8, [[zdst_w, rows], [1, ncol]]),
                            w_zb[0:rows, 0:ncol], waits=[("vP", v_z)], sem="z3s")

        v_un = v_z
        t_bc = 0
        v_bcE = 0
        d_st = 0

        def writer_block(rows, gidx_col_ap, iota_ap, gtile, widths, qw_t, qw_rowstride,
                         qw_off, sc_t, sc_rowstride, sc_off, store_dst_fn, dma_rows=None):
            nonlocal v_un, t_bc, v_bcE, d_st
            vG = mark("vector", lambda eng: eng.tensor_scalar(
                gtile, iota_ap, gidx_col_ap, None, OP.is_equal),
                waits=[d_consts, ("tP", P.v("tP"))])
            tG = mark("tensor", lambda eng: eng.transpose(
                pbf(7, rows, 128), gtile, s_ident), waits=[("vP", vG)])
            vGT = mark("vector", lambda eng: eng.tensor_copy(
                s_GT[0:rows, 0:128], pbf(7, rows, 128)), waits=[("tP", tG)])
            j0 = 0
            for sl, jw in enumerate(widths):
                dq = 0
                for rpl in range(8):
                    dst = bass.AP(acast[U32], wb // 4 + rpl * (ARENA_B // 4),
                                  [[(ARENA_B // 4) * 8, 16], [1, jw]])
                    dq = dma(dst, bass.AP(qw_t, qw_off + j0,
                                          [[qw_rowstride, 16], [1, jw]]),
                             waits=[("vP", v_un)] if rpl == 0 else [], sem="dq")
                drows = dma_rows or rows
                dsc = dma(w_sc[0:drows, 0:jw],
                          bass.AP(sc_t, sc_off + j0, [[sc_rowstride, drows], [1, jw]]),
                          waits=[("vP", P.v("vP"))], sem="ds1")
                vscb = mark("vector", lambda eng, r=rows, n=jw: eng.tensor_copy(
                    w_scb[0:r, 0:n], w_sc[0:r, 0:n]), waits=[("ds1", dsc), ("tP", t_bc)])
                for q4 in range(jw // 512):
                    t_bc = mark("tensor", lambda eng, q=q4, r=rows: eng.matmul(
                        pb(6), s_GT[0:r, 0:128], w_scb[0:r, q * 512:(q + 1) * 512],
                        start=True, stop=True),
                        waits=[("vP", max(vscb, vGT, v_bcE))])
                    v_bcE = mark("vector", lambda eng, q=q4: eng.tensor_copy(
                        w_bc[:, q * 512:(q + 1) * 512], pb(6)), waits=[("tP", t_bc)])
                v_un = mark("vector", lambda eng, n=jw: eng.tensor_scalar(
                    w_nib[:, 0:n], w_qw[:, 0:n], s_shift, 15,
                    OP.logical_shift_right, OP.bitwise_and),
                    waits=[("dq", dq)])
                v_w1 = mark("vector", lambda eng, n=jw: eng.tensor_tensor(
                    w_w1[:, 0:n], w_nib[:, 0:n], w_bc[:, 0:n], OP.mult),
                    waits=[("ds2", d_st), ("vP", v_bcE)])
                d_st = dma(store_dst_fn(j0), vap(wb + 20480, BF16, 128,
                                                 [(128, jw // 128), (1, 128)]) if rows == 32
                           else vap(wb + 20480, BF16, 128, [(512, jw // 512), (1, 512)]),
                           waits=[("vP", v_w1)], sem="ds2")
                j0 += jw

        for it in range(4):
            writer_block(
                32, s_gidxgw[:, it:it + 1], s_iota32, s_G32, [2048] * 10 + [1536],
                qw_gu.tensor, 22016, it * 16 * 22016,
                sc_gu.tensor, 22016, 0,
                lambda j0, _it=it, : bass.AP(
                    guW, _it * (JT * 16384) + (j0 // 128) * 16384,
                    [[128, 128], [16384, (2048 if j0 + 2048 <= 22016 else 1536) // 128], [1, 128]]))
        for k in range(11):
            writer_block(
                86, s_gidxdw[:, k:k + 1], s_iota86, s_G86, [2048, 2048],
                qw_dn.tensor, 4096, k * 16 * 4096,
                sc_dn.tensor, 4096, 0,
                lambda j0, _k=k: bass.AP(
                    dnW, _k * (8 * 65536) + (j0 // 512) * 65536,
                    [[512, 128], [65536, 4], [1, 512]]), dma_rows=96)
        wr_done = [(nm, P.v(nm)) for nm in
                   ("dq", "ds1", "ds2", "z1s", "z2s", "z3s")]
        v_wr_done = P.v("vP")

        P.op("gpsimd", lambda eng: eng.collective_compute(
            "AllGather", OP.bypass, replica_groups=[list(range(8))],
            ins=[guW.ap().opt()], outs=[guWall.ap().opt()]),
            waits=list(wr_done), inc="c1")
        P.op("gpsimd", lambda eng: eng.collective_compute(
            "AllGather", OP.bypass, replica_groups=[list(range(8))],
            ins=[dnW.ap().opt()], outs=[dnWall.ap().opt()]), inc="c2")
        P.op("gpsimd", lambda eng: eng.collective_compute(
            "AllReduce", OP.add, replica_groups=[list(range(8))],
            ins=[bar_i.ap().opt()], outs=[bar_o.ap().opt()]), inc="c3")

        dump("guw", bass.AP(guW, 0, [[JT * 16384, 4], [1, JT * 16384]]),
             4, JT * 16384, BF16, waits=wr_done + [("vP", v_wr_done)])
        dump("dnw", bass.AP(dnW, 0, [[8 * 65536, 11], [1, 8 * 65536]]),
             11, 8 * 65536, BF16, waits=list(wr_done))
        dump("zg", bass.AP(zg_d, 0, [[22016, 32], [1, 22016]]),
             32, 22016, BF16, waits=list(wr_done))
        dump("zd", bass.AP(zd_d, 0, [[4096, 86], [1, 4096]]),
             86, 4096, BF16, waits=list(wr_done))

        state = {
            "wf_free": [0, 0, 0],
            "wf_sem": ["vP", "vP", "vP"],
            "wbf_free": [0, 0, 0, 0],
            "bank_free": [0] * 8,
            "bank_sem": ["vP"] * 8,
            "nconv": 0,
        }

        def ln_phase(ch, which, extra_first=()):
            first_w = list(extra_first) + [d_consts, ("dG", d_x)]
            t_s1 = t_s2 = 0
            a_sq = 0
            for ct in range(32):
                a_sq = mark("scalar", lambda eng, c=ct: eng.activation(
                    s_sqt[c % 2], Xt(c, ch), AF.Square),
                    waits=(first_w if ct == 0 else [("tP", t_s2)]))
                w0 = []
                if ct == 0:
                    w0 = first_w + [(state["bank_sem"][i], state["bank_free"][i])
                                    for i in (0, 1, 2)]
                t_s1 = mark("tensor", lambda eng, c=ct: eng.matmul(
                    pb(0, 1, 512), s_onescol, Xt(c, ch),
                    start=(c == 0), stop=(c == 31)), waits=w0)
                t_s2 = mark("tensor", lambda eng, c=ct: eng.matmul(
                    pb(1, 1, 512), s_onescol, s_sqt[c % 2],
                    start=(c == 0), stop=(c == 31)), waits=[("aP", a_sq)])
            quiet("vector", lambda eng: eng.tensor_scalar(
                s_row[0], pb(0, 1, 512), 1.0 / HID, None, OP.mult),
                waits=[("tP", t_s1)])
            quiet("vector", lambda eng: eng.tensor_copy(s_mrow, s_row[0]))
            quiet("vector", lambda eng: eng.tensor_scalar(
                s_row[1], pb(1, 1, 512), 1.0 / HID, None, OP.mult),
                waits=[("tP", t_s2)])
            quiet("vector", lambda eng: eng.tensor_tensor(
                s_row[2], s_row[0], s_row[0], OP.mult))
            v_var = mark("vector", lambda eng: eng.tensor_tensor(
                s_row[1], s_row[1], s_row[2], OP.subtract))
            a_sqr = mark("scalar", lambda eng: eng.activation(
                s_row[2], s_row[1], AF.Sqrt, bias=EPS), waits=[("vP", v_var)])
            quiet("vector", lambda eng: eng.tensor_copy(s_rowbf[0], s_row[2]),
                  waits=[("aP", a_sqr)])
            quiet("vector", lambda eng: eng.reciprocal(s_row[1], s_row[2]))
            quiet("vector", lambda eng: eng.tensor_copy(s_rowbf[1], s_row[1]))
            quiet("vector", lambda eng: eng.scalar_tensor_tensor(
                s_row[2], s_row[0], -1.0, s_row[1], OP.mult, OP.mult))
            v_rows = mark("vector", lambda eng: eng.tensor_copy(s_rowbf[2], s_row[2]))
            v_bc = 0
            for slot, rb in ((0, 1), (1, 2), (2, 0)):
                t_b = mark("tensor", lambda eng, r=rb: eng.matmul(
                    pb(2), s_onesrow, s_rowbf[r], start=True, stop=True),
                    waits=[("vP", max(v_rows, v_bc))])
                v_bc = mark("vector", lambda eng, sl=slot: eng.tensor_copy(
                    s_pool[sl], pb(2)), waits=[("tP", t_b)])
            for i in (0, 1, 2):
                state["bank_free"][i] = v_bc
                state["bank_sem"][i] = "vP"
            v_tr = 0
            for ct in range(32):
                quiet("vector", lambda eng, c=ct: eng.tensor_tensor(
                    s_tmpf, Xt(c, ch), s_pool[0], OP.mult),
                    waits=[("vP", v_bc)] if ct == 0 else [])
                v_tr = mark("vector", lambda eng, c=ct: eng.tensor_tensor(
                    Xt(c, ch), s_tmpf, s_pool[1], OP.add))
            return v_tr

        def proj_phase(ch, p, rhs_base, dst_base, extra_mm_wait=(), extra_evac_wait=(),
                       fold_mrow=False, recon=False):
            v_ev = 0
            for og in range(8):
                pset = (og % 2) * 4
                tmm = 0
                for ct in range(32):
                    n = state["nconv"]
                    i3 = n % 3
                    j4 = n % 4
                    state["nconv"] = n + 1
                    if PROJ_SERIAL == "tP":
                        ser = [("tP", P.v("tP"))]
                    elif PROJ_SERIAL == "vP":
                        ser = [("vP", P.v("vP"))]
                    elif PROJ_SERIAL:
                        ser = [("tP", P.v("tP")), ("vP", P.v("vP"))]
                    else:
                        ser = []
                    dW = dma(s_wf32[i3],
                             bass.AP(WTd[p].tensor, ct * 128 * HID + og * 512,
                                     [[HID, 128], [1, 512]]),
                             waits=[(state["wf_sem"][i3], state["wf_free"][i3])] + ser,
                             sem=f"w{i3}")
                    ceng = "gpsimd" if (GP_CONVERT and n % 2 == 0) else "vector"
                    cm = mark(ceng, lambda eng, a=j4, b=i3: eng.tensor_copy(
                        s_wbf[a], s_wf32[b]),
                        waits=[(f"w{i3}", dW), ("tP", state["wbf_free"][j4])])
                    state["wf_free"][i3] = cm
                    state["wf_sem"][i3] = "gP" if ceng == "gpsimd" else "vP"
                    csem = state["wf_sem"][i3]
                    if rhs_base is None:
                        rhs = Xt(ct, ch)
                    else:
                        rhs = bass.AP(acast[BF16], rhs_base + ct * 128,
                                      [[ARENA_B // 2, 128], [4096, 4], [1, 128]])
                    for ot in range(4):
                        bk = pset + ot
                        w = [(csem, cm)] if ot == 0 else []
                        if ct == 0:
                            w.append((state["bank_sem"][bk], state["bank_free"][bk]))
                            w += list(extra_mm_wait)
                        tmm = mark("tensor", lambda eng, b=bk, j=j4, o=ot, r=rhs, c=ct: eng.matmul(
                            pb(b), s_wbf[j][:, o * 128:(o + 1) * 128], r,
                            start=(c == 0), stop=(c == 31 and not fold_mrow)), waits=w)
                    state["wbf_free"][j4] = tmm
                if fold_mrow:
                    for ot in range(4):
                        tmm = mark("tensor", lambda eng, b=pset + ot: eng.matmul(
                            pb(b), s_onesrow, s_mrow, start=False, stop=True))
                for ot in range(4):
                    bk = pset + ot
                    o = og * 4 + ot
                    ew = list(extra_evac_wait) if (og == 0 and ot == 0) else []
                    if not recon:
                        dstap = bass.AP(acast[BF16], dst_base + o,
                                        [[ARENA_B // 2, 128], [128, 128], [32, 4]])
                        srcap = bass.AP(bank[bk], 0, [[512, 128], [4, 128], [1, 4]])
                        v_ev = mark("vector", lambda eng, sp=srcap, oo=o, d=dstap: eng.tensor_scalar(
                            d, sp, s_bT[p][:, oo:oo + 1], None, OP.add),
                            waits=[("tP", tmm)] + ew)
                    else:
                        quiet("vector", lambda eng, oo=o: eng.tensor_tensor(
                            s_tmpf, Xt(oo, ch), s_pool[2], OP.mult),
                            waits=[("tP", tmm)] + ew)
                        v_ev = mark("vector", lambda eng, b=bk, oo=o: eng.scalar_tensor_tensor(
                            Xt(oo, ch), pb(b), s_bT[p][:, oo:oo + 1], s_tmpf,
                            OP.add, OP.add))
                    state["bank_free"][bk] = v_ev
                    state["bank_sem"][bk] = "vP"
                if og == 0 and p == "q" and ch == 0:
                    wv = [("vP", P.v("vP"))]
                    dump("qog0", view(o_BIG, 128, 16384, BF16), 128, 16384,
                         BF16, waits=wv)
                    dump("zwf", view(o_ZONE, 128, 1536, F32), 128, 1536,
                         F32, waits=wv)
                    dump("zwb", view(o_ZONE + 6144, 128, 2048, BF16), 128, 2048,
                         BF16, waits=wv)
            return v_ev

        def attn_core(ch, v_qkv_done, extra_first=()):
            t_sc = {}
            t_pt = {}
            t_vb = {}
            t_av = {}
            m_sstt = {}
            m_exp = {}
            m_v0 = {}
            m_v1 = {}
            m_v2 = {}
            m_v3 = {}
            v_deint = 0
            for G in range(4):
                g0 = G * 32
                for b in range(33):
                    gb = g0 + b
                    if b < 32:
                        r = gb % 2
                        qb = bass.AP(acast[BF16], QTB + gb * 128,
                                     [[ARENA_B // 2, 128], [1, 128]])
                        kb = bass.AP(acast[BF16], KTB + gb * 128,
                                     [[ARENA_B // 2, 128], [1, 128]])
                        vbp = bass.AP(acast[BF16], VTB + gb * 128,
                                      [[ARENA_B // 2, 128], [1, 128]])
                        w = [("vP", m_sstt.get(gb - 2, 0))]
                        if gb == 0:
                            w += [("vP", v_qkv_done)] + list(extra_first)
                        t_sc[gb] = mark("tensor", lambda eng, q=qb, k=kb, r2=r: eng.matmul(
                            pb(0 + r2, 128, 128), q, k, start=True, stop=True), waits=w)
                        t_vb[gb] = mark("tensor", lambda eng, v=vbp, r2=r: eng.transpose(
                            pbf(4 + r2, 128, 128), v, s_ident),
                            waits=[("vP", m_v2.get(gb - 2, 0))])
                    if b >= 1:
                        gbb = g0 + b - 1
                        r2 = gbb % 2
                        t_pt[gbb] = mark("tensor", lambda eng, r3=r2: eng.transpose(
                            pbf(2 + r3, 128, 128), s_eS[r3], s_ident),
                            waits=[("vP", m_v0[gbb]), ("vP", m_v1.get(gbb - 2, 0))])
                        m_v1[gbb] = mark("vector", lambda eng, r3=r2: eng.tensor_copy(
                            s_pTS[r3], pbf(2 + r3, 128, 128)),
                            waits=[("tP", t_pt[gbb]), ("tP", t_av.get(gbb - 2, 0))])
                        m_v2[gbb] = mark("vector", lambda eng, r3=r2: eng.tensor_copy(
                            s_vbS[r3], pbf(4 + r3, 128, 128)),
                            waits=[("tP", t_vb[gbb])])
                        t_av[gbb] = mark("tensor", lambda eng, r3=r2: eng.matmul(
                            pb(6 + r3, 128, 128), s_vbS[r3], s_pTS[r3],
                            start=True, stop=True),
                            waits=[("vP", m_v2[gbb]),
                                   ("vP", m_v3.get(gbb - 2, 0))])
                    if b < 32:
                        r2 = gb % 2
                        m_sstt[gb] = mark("vector", lambda eng, r3=r2: eng.scalar_tensor_tensor(
                            s_sS[r3], pb(0 + r3, 128, 128), ISD, s_mask, OP.mult, OP.add),
                            waits=[("tP", t_sc[gb]), ("aP", m_exp.get(gb - 2, 0))])
                        m_exp[gb] = mark("scalar", lambda eng, r3=r2: eng.activation(
                            s_eS[r3], s_sS[r3], AF.Exp),
                            waits=[("vP", m_sstt[gb]),
                                   ("tP", t_pt.get(gb - 2, 0))])
                        quiet("vector", lambda eng, r3=r2: eng.scalar_tensor_tensor(
                            s_pnS[r3], s_sS[r3], THRESH, s_eS[r3], OP.is_gt, OP.mult),
                            waits=[("aP", m_exp[gb])])
                        quiet("vector", lambda eng, r3=r2: eng.reduce_sum(
                            s_rS[:, r3:r3 + 1], s_pnS[r3], axis=AX.X))
                        quiet("vector", lambda eng, r3=r2: eng.reciprocal(
                            s_rS[:, 2 + r3:3 + r3], s_rS[:, r3:r3 + 1]))
                        m_v0[gb] = mark("vector", lambda eng, r3=r2: eng.tensor_scalar(
                            s_eS[r3], s_pnS[r3], s_rS[:, 2 + r3:3 + r3], None, OP.mult))
                    if b >= 1:
                        gbb = g0 + b - 1
                        r2 = gbb % 2
                        m_v3[gbb] = mark("vector", lambda eng, r3=r2, b2=b - 1: eng.tensor_copy(
                            vap(o_stage + b2 * 256, BF16, 128, [(1, 128)]),
                            pb(6 + r3, 128, 128)),
                            waits=[("tP", t_av[gbb])])
                for h in range(32):
                    src = bass.AP(acast[BF16], o_stage // 2 + h,
                                  [[ARENA_B // 2, 128], [128, 32], [32, 4]])
                    dst = bass.AP(acast[BF16], KTB + G * 4096 + h * 128,
                                  [[ARENA_B // 2, 128], [4, 32], [1, 4]])
                    hw = [("tP", t_sc[g0 + 31]), ("vP", m_v3[g0 + 31])] if h == 0 else []
                    if h == 31:
                        v_deint = mark("vector",
                                       lambda eng, s2=src, d2=dst: eng.tensor_copy(d2, s2),
                                       waits=hw)
                    else:
                        quiet("vector", lambda eng, s2=src, d2=dst: eng.tensor_copy(d2, s2),
                              waits=hw)
            for i in range(8):
                state["bank_free"][i] = P.v("vP")
                state["bank_sem"][i] = "vP"
            return v_deint

        def ffn_phase(ch, v_ln2_done, first_gu_waits):
            t_x = 0
            for ct in range(32):
                vg = mark("vector", lambda eng, c=ct: eng.tensor_scalar(
                    s_G32, s_iota32, s_gidxg[:, c:c + 1], None, OP.is_equal),
                    waits=[("tP", t_x), ("vP", v_ln2_done if ct == 0 else 0)])
                wx = [("vP", vg)]
                if ct == 0:
                    wx.append((state["bank_sem"][4], state["bank_free"][4]))
                t_x = mark("tensor", lambda eng, c=ct: eng.matmul(
                    pb(4, 32, 512), s_G32, Xt(c, ch), start=(c == 0), stop=(c == 31)),
                    waits=wx)
            v_xg = mark("vector", lambda eng: eng.tensor_copy(s_xg, pb(4, 32, 512)),
                        waits=[("tP", t_x)])
            slab_free = [0, 0]
            gate_free = [0, 0]
            up_free = [0, 0]
            zg_free = [0, 0]
            a_silu = {}
            v_h = {}
            d_zg = {}
            for idx in range(2 * FT):
                ft = idx // 2
                is_up = idx % 2
                jt = ft + 86 * is_up
                sl = idx % 2
                w = [("tP", slab_free[sl])]
                if idx == 0:
                    w += list(first_gu_waits) + [("c1", 1), ("c3", 1)]
                src = bass.AP(guWall, jt * 16384,
                              [[128, 128], [JT * 16384, 32], [1, 128]])
                gsem = "ga" if sl == 0 else "gb"
                dsl = dma(vap(z + sl * 8192, BF16, 128, [(128, 32), (1, 128)]), src,
                          waits=w, sem=gsem)
                if not is_up:
                    zsem = "za" if ft % 2 == 0 else "zb"
                    z1 = dma(s_zg[ft % 2][0:32, 0:128],
                             bass.AP(zg_d, ft * 128, [[22016, 32], [1, 128]]),
                             waits=[("tP", zg_free[ft % 2])], sem=zsem)
                    z2 = dma(s_zg[ft % 2][0:32, 128:256],
                             bass.AP(zg_d, (86 + ft) * 128, [[22016, 32], [1, 128]]),
                             sem=zsem)
                    d_zg[ft] = (zsem, z2)
                bk = (2 if is_up else 0) + ft % 2
                wmm0 = [(gsem, dsl)]
                if not is_up:
                    wmm0.append(("aP", gate_free[ft % 2]))
                else:
                    wmm0.append(("vP", up_free[ft % 2]))
                tmm = 0
                for ct in range(32):
                    tmm = mark("tensor", lambda eng, b=bk, c=ct, s2=sl: eng.matmul(
                        pb(b), s_guslab[s2][:, c * 128:(c + 1) * 128], Xt(c, ch),
                        start=(c == 0), stop=False),
                        waits=(wmm0 if ct == 0 else []))
                tz = mark("tensor", lambda eng, b=bk, f=ft, u=is_up: eng.matmul(
                    pb(b), s_zg[f % 2][0:32, u * 128:(u + 1) * 128], s_xg,
                    start=False, stop=True),
                    waits=[d_zg[ft], ("vP", v_xg)])
                slab_free[sl] = tz
                if not is_up:
                    a_silu[ft] = mark("scalar", lambda eng, b=bk, f=ft: eng.activation(
                        s_gtmp[f % 2], pb(b), AF.Silu),
                        waits=[("tP", tz), ("vP", v_h.get(ft - 2, 0))])
                    gate_free[ft % 2] = a_silu[ft]
                else:
                    v_h[ft] = mark("vector", lambda eng, b=bk, f=ft: eng.tensor_tensor(
                        Hbt(f), s_gtmp[f % 2], pb(b), OP.mult),
                        waits=[("tP", tz), ("aP", a_silu[ft])])
                    up_free[ft % 2] = v_h[ft]
                    zg_free[ft % 2] = tz
            state["bank_free"][0] = a_silu[84]; state["bank_sem"][0] = "aP"
            state["bank_free"][1] = a_silu[85]; state["bank_sem"][1] = "aP"
            state["bank_free"][2] = v_h[84]; state["bank_sem"][2] = "vP"
            state["bank_free"][3] = v_h[85]; state["bank_sem"][3] = "vP"
            t_hg = 0
            for ft in range(FT):
                vg = mark("vector", lambda eng, c=ft: eng.tensor_scalar(
                    s_G86, s_iota86, s_gidxd[:, c:c + 1], None, OP.is_equal),
                    waits=[("tP", t_hg)])
                t_hg = mark("tensor", lambda eng, c=ft: eng.matmul(
                    pb(4, 86, 512), s_G86, Hbt(c), start=(c == 0), stop=(c == 85)),
                    waits=[("vP", max(vg, v_h[ft]))] +
                          ([("vP", v_xg)] if ft == 0 else []))
            v_hg = mark("vector", lambda eng: eng.tensor_copy(s_hg, pb(4, 86, 512)),
                        waits=[("tP", t_hg)])
            state["bank_free"][4] = v_hg
            state["bank_sem"][4] = "vP"
            dn_free = [0, 0, 0, 0]
            zd_free = [0, 0]
            v_ev = 0
            for og in range(8):
                pset = (og % 2) * 4
                esem = "ea" if og % 2 == 0 else "eb"
                d_zd = dma(s_zd[og % 2],
                           bass.AP(zd_d, og * 512, [[4096, 96], [1, 512]]),
                           waits=[("tP", zd_free[og % 2])], sem=esem)
                for ot in range(4):
                    bk = pset + ot
                    mark("tensor", lambda eng, b=bk: eng.matmul(
                        pb(b), s_onesrow, s_mrow, start=True, stop=False),
                        waits=[(state["bank_sem"][bk], state["bank_free"][bk])] +
                              ([("vP", v_hg)] if og == 0 and ot == 0 else []))
                tmm = 0
                for ft in range(FT):
                    k4 = (og * FT + ft) % 4
                    nsem = ("na", "nb", "nc2", "nd")[k4]
                    dsl = dma(s_dnslab[k4],
                              bass.AP(dnWall, ft * (8 * 65536) + og * 65536,
                                      [[512, 128], [1, 512]]),
                              waits=[("tP", dn_free[k4])] +
                                    ([("c2", 1)] if og == 0 and ft == 0 else []),
                              sem=nsem)
                    for ot in range(4):
                        bk = pset + ot
                        tmm = mark("tensor", lambda eng, b=bk, k=k4, o=ot, f=ft: eng.matmul(
                            pb(b), s_dnslab[k][:, o * 128:(o + 1) * 128], Hbt(f),
                            start=False, stop=False),
                            waits=[(nsem, dsl)] if ot == 0 else [])
                    dn_free[k4] = tmm
                tz = 0
                for ot in range(4):
                    bk = pset + ot
                    tz = mark("tensor", lambda eng, b=bk, o=ot, g=og: eng.matmul(
                        pb(b), s_zd[g % 2][0:86, o * 128:(o + 1) * 128], s_hg,
                        start=False, stop=True),
                        waits=[(esem, d_zd), ("vP", v_hg)] if ot == 0 else [])
                zd_free[og % 2] = tz
                for ot in range(4):
                    bk = pset + ot
                    o = og * 4 + ot
                    quiet("vector", lambda eng, oo=o: eng.tensor_tensor(
                        s_tmpf, Xt(oo, ch), s_pool[2], OP.mult), waits=[("tP", tz)])
                    v_ev = mark("vector", lambda eng, b=bk, oo=o: eng.tensor_tensor(
                        Xt(oo, ch), pb(b), s_tmpf, OP.add))
                    state["bank_free"][bk] = v_ev
                    state["bank_sem"][bk] = "vP"
            return v_ev

        def out_phase(ch, v_x2_done):
            st_free = [0, 0]
            for tt in range(4):
                for g4 in range(8):
                    r = g4 % 2
                    t_tr = 0
                    for ci in range(4):
                        ct = g4 * 4 + ci
                        w = []
                        if ci == 0:
                            w = [("vP", v_x2_done),
                                 (state["bank_sem"][r], state["bank_free"][r])]
                        t_tr = mark("tensor", lambda eng, c=ct, i=ci, r2=r,
                                    c0=ch * CH + tt * 128: eng.transpose(
                            pbf(r2, 128, 128, col0=i * 128),
                            Xcol(c, c0, 128), s_ident), waits=w)
                    osem = "oa" if r == 0 else "ob"
                    v_e = mark("vector", lambda eng, r2=r: eng.tensor_copy(
                        s_outb[r2], pbf(r2, 128, 512)),
                        waits=[("tP", t_tr), (osem, st_free[r])])
                    state["bank_free"][r] = v_e
                    state["bank_sem"][r] = "vP"
                    dst = bass.AP(out.tensor, (ch * CH + tt * 128) * HID + g4 * 512,
                                  [[HID, 128], [1, 512]])
                    P.op("sync", lambda eng, d=dst, r2=r: eng.dma_start(d, s_outb[r2]),
                         waits=[("vP", v_e)], inc=osem, inc_by=16)
                    st_free[r] = P.v(osem)

        prev = {"t_dn_last": 0}
        for ch in range(2):
            if ch == 0:
                evac_gate = list(wr_done)
                ln1_first = []
            else:
                evac_gate = [("tP", prev["t_dn_last"])]
                ln1_first = [("tP", prev["t_dn_last"])]
            v_ln1 = ln_phase(ch, 1, extra_first=ln1_first)
            if ch == 0:
                dump("ln1", view(o_X, 128, 32768, BF16), 128, 32768, BF16,
                     waits=[("vP", v_ln1)])
                dump("rows1", view(o_rowz, 1, 1536, F32), 1, 1536, F32,
                     waits=[("vP", v_ln1)])
                dump("pool1", view(o_pool, 128, 1536, F32), 128, 1536, F32,
                     waits=[("vP", v_ln1)])
            v_q = proj_phase(ch, "q", None, QTB, extra_mm_wait=[("vP", v_ln1)],
                             extra_evac_wait=evac_gate)
            if ch == 0:
                dump("qbuf", view(o_BIG, 128, 16384, BF16), 128, 16384, BF16,
                     waits=[("vP", v_q)])
                dump("qbuf2", view(o_BIG, 128, 16384, BF16), 128, 16384, BF16,
                     waits=[("vP", v_q)])
            v_k = proj_phase(ch, "k", None, KTB)
            v_v = proj_phase(ch, "v", None, VTB)
            if ch == 0:
                dump("kbuf", view(o_BIG + 32768, 128, 16384, BF16), 128, 16384,
                     BF16, waits=[("vP", v_v)])
                dump("vbuf", view(o_BIG + 65536, 128, 16384, BF16), 128, 16384,
                     BF16, waits=[("vP", v_v)])
            core_extra = [("tP", prev["t_dn_last"])] if ch == 1 else []
            v_deint = attn_core(ch, v_v, extra_first=core_extra)
            if ch == 0:
                dump("abuf", view(o_BIG + 32768, 128, 16384, BF16), 128, 16384,
                     BF16, waits=[("vP", v_deint)])
            v_wo = proj_phase(ch, "o", KTB, None, fold_mrow=True, recon=True,
                              extra_mm_wait=[("vP", v_deint)])
            if ch == 0:
                dump("x1", view(o_X, 128, 32768, BF16), 128, 32768, BF16,
                     waits=[("vP", v_wo)])
            v_ln2 = ln_phase(ch, 2, extra_first=[("vP", v_wo)])
            if ch == 0:
                dump("ln2", view(o_X, 128, 32768, BF16), 128, 32768, BF16,
                     waits=[("vP", v_ln2)])
                dump("rows2", view(o_rowz, 1, 1536, F32), 1, 1536, F32,
                     waits=[("vP", v_ln2)])
            first_gu = [("vP", v_deint), ("tP", P.v("tP"))]
            v_x2 = ffn_phase(ch, v_ln2, first_gu)
            if ch == 0:
                dump("xgbuf", view(o_xg, 32, 512, BF16), 32, 512, BF16,
                     waits=[("vP", v_x2)])
                dump("hgbuf", view(o_hg, 86, 512, BF16), 86, 512, BF16,
                     waits=[("vP", v_x2)])
                dump("hbuf", view(o_BIG, 128, 86 * 512, BF16), 128, 86 * 512,
                     BF16, waits=[("vP", v_x2)])
                dump("x2", view(o_X, 128, 32768, BF16), 128, 32768, BF16,
                     waits=[("vP", v_x2)])
            prev["t_dn_last"] = P.v("tP")
            out_phase(ch, v_x2)

        P.op("sync", lambda eng: None,
             waits=[("oa", P.v("oa")), ("ob", P.v("ob"))])

        for nm in dbg_outs:
            pass

        with nc.Block() as block:
            @block.sync
            def _(eng):
                for e, f in P.ops:
                    if e == "sync":
                        f(eng)

            @block.tensor
            def _(eng):
                for e, f in P.ops:
                    if e == "tensor":
                        f(eng)

            @block.vector
            def _(eng):
                for e, f in P.ops:
                    if e == "vector":
                        f(eng)

            @block.scalar
            def _(eng):
                for e, f in P.ops:
                    if e == "scalar":
                        f(eng)

            @block.gpsimd
            def _(eng):
                for e, f in P.ops:
                    if e == "gpsimd":
                        f(eng)

    return nc


def kernel(**inputs):
    f32 = np.float32
    x = np.asarray(inputs["L0_attn_x"], f32).reshape(8192, HID)
    xT_full = np.ascontiguousarray(x.T)
    WT = {p: np.ascontiguousarray(np.asarray(inputs[f"L0_attn_W{p}"], f32).T)
          for p in "qkvo"}
    bT = {p: np.ascontiguousarray(np.asarray(inputs[f"L0_attn_b{p}"], f32)
                                  .reshape(32, 128).T) for p in "qkvo"}
    qwg = np.asarray(inputs["L0_ffn_qweight_gu"], np.uint32)
    qzg = np.asarray(inputs["L0_ffn_qzeros_gu"], np.uint32)
    scg = np.asarray(inputs["L0_ffn_scales_gu"], f32)
    gig = np.asarray(inputs["L0_ffn_g_idx_gu"]).astype(f32)
    qwd = np.asarray(inputs["L0_ffn_qweight_down"], np.uint32)
    qzd = np.asarray(inputs["L0_ffn_qzeros_down"], np.uint32)
    scd = np.asarray(inputs["L0_ffn_scales_down"], f32)
    gid = np.asarray(inputs["L0_ffn_g_idx_down"]).astype(f32)

    gidxg = np.ascontiguousarray(gig.reshape(32, 128).T)
    gidxd = np.ascontiguousarray(gid.reshape(86, 128).T)
    qwd_pad = np.zeros((8 * 176, 4096), np.uint32)
    qwd_pad[:1376] = qwd
    qzd_pad = np.zeros((96, 512), np.uint32)
    qzd_pad[:86] = qzd
    scd_pad = np.zeros((96, 4096), f32)
    scd_pad[:86] = scd
    gidxd_pad = np.zeros((128, 88), f32)
    gidxd_pad[:, :86] = gidxd

    nc = build_program()
    in_maps = []
    for c in range(8):
        m = dict(
            xT=np.ascontiguousarray(xT_full[:, c * T:(c + 1) * T]),
            qw_gu=np.ascontiguousarray(qwg[c * 64:(c + 1) * 64]),
            qz_gu=qzg, sc_gu=scg,
            gidx_gu=gidxg,
            gidx_guw=np.ascontiguousarray(gidxg[:, c * 4:(c + 1) * 4]),
            qw_dn=np.ascontiguousarray(qwd_pad[c * 176:(c + 1) * 176]),
            qz_dn=qzd_pad, sc_dn=scd_pad,
            gidx_dn=gidxd,
            gidx_dnw=np.ascontiguousarray(gidxd_pad[:, c * 11:(c + 1) * 11]),
        )
        for p in "qkvo":
            m[f"W{p}T"] = WT[p]
            m[f"b{p}T"] = bT[p]
        in_maps.append(m)
    res = run_bass_kernel_spmd(nc, in_maps, core_ids=list(range(8)))
    if DUMP:
        DBG_RESULTS.clear()
        for c in range(8):
            DBG_RESULTS.append({n: np.asarray(res.results[c]["dbg_" + n])
                                for n in build_program.dbg_names})
    parts = [np.asarray(res.results[c]["out"]) for c in range(8)]
    return np.concatenate(parts, axis=0).reshape(4, 2048, HID).astype(np.float32)


# version 11
# speedup vs baseline: 1.0060x; 1.0060x over previous
import math
import numpy as np
import ml_dtypes
import concourse.bass as bass
import concourse.mybir as mybir
from concourse.bass_utils import run_bass_kernel_spmd

F32 = mybir.dt.float32
BF16 = mybir.dt.bfloat16
U32 = mybir.dt.uint32
AF = mybir.ActivationFunctionType
OP = mybir.AluOpType
AX = mybir.AxisListType
BF = ml_dtypes.bfloat16

DUMP = set()
WR_LIMIT = 10**9
WR_NO_BC = False
WR_NO_EXT = False
WR_NO_STORE = False
WR_NO_GID = False
DBG_RESULTS = []

T = 1024
CH = 512
HID = 4096
JT = 172
FT = 86
ISD = 1.0 / math.sqrt(128.0)
THRESH = 0.001
EPS = 1e-5
GP_CONVERT = False
PROJ_SERIAL = False


ESEM = {"tensor": "tP", "vector": "vP", "scalar": "aP", "gpsimd": "gP"}


class Prog:

    def __init__(self):
        self.ops = []
        self.cnt = {}
        self.handles = {}

    def v(self, name):
        return self.cnt.get(name, 0)

    def op(self, engine, fn, waits=(), inc=None, inc_by=1, barrier=False):
        waits = list(waits)
        if engine in ("vector", "scalar", "gpsimd") and not barrier:
            sem = ESEM[engine]
            prev = self.v(sem)
            if prev > 0:
                waits.append((sem, prev))
            if inc is None:
                inc, inc_by = sem, 1
        waits = [(s, v) for (s, v) in waits if v > 0]
        tgt = None
        if inc is not None:
            self.cnt[inc] = self.cnt.get(inc, 0) + inc_by
            tgt = (inc, inc_by)

        def emit(eng, _fn=fn, _w=waits, _t=tgt, _P=self):
            for s, vv in _w:
                eng.wait_ge(_P.handles[s], vv)
            inst = _fn(eng)
            if _t is not None:
                inst.then_inc(_P.handles[_t[0]], _t[1])
        self.ops.append((engine, emit))
        return self.v(inc) if inc else None


def build_program(dbg=()):
    nc = bass.Bass("TRN2", target_bir_lowering=False, debug=False,
                   dynamic_dma_scratch_size=8192)
    P = Prog()

    def inp(name, shape, dt):
        return nc.dram_tensor(name, list(shape), dt, kind="ExternalInput").ap()

    xT = inp("xT", (HID, T), F32)
    WTd = {p: inp(f"W{p}T", (HID, HID), F32) for p in "qkvo"}
    bTd = {p: inp(f"b{p}T", (128, 32), F32) for p in "qkvo"}
    qw_gu = inp("qw_gu", (64, 22016), U32)
    qz_gu = inp("qz_gu", (32, 2752), U32)
    sc_gu = inp("sc_gu", (32, 22016), F32)
    gidx_gu = inp("gidx_gu", (128, 32), F32)
    gidx_guT = inp("gidx_guT", (4, 128), F32)
    qw_dn = inp("qw_dn", (176, 4096), U32)
    qz_dn = inp("qz_dn", (96, 512), U32)
    sc_dn = inp("sc_dn", (96, 4096), F32)
    gidx_dn = inp("gidx_dn", (128, 86), F32)
    gidx_dnT = inp("gidx_dnT", (11, 128), F32)

    out = nc.dram_tensor("out", [T, HID], F32, kind="ExternalOutput").ap()

    mask_np = np.full((128, 128), -1e30, np.float32)
    for b in range(4):
        mask_np[b * 32:(b + 1) * 32, b * 32:(b + 1) * 32] = 0.0
    c_ident = nc.inline_tensor(np.eye(128, dtype=BF), "c_ident").ap()
    c_mask = nc.inline_tensor(mask_np, "c_mask").ap()
    c_iota32 = nc.inline_tensor(np.tile(np.arange(32, dtype=np.float32), (128, 1)), "c_iota32").ap()
    c_iota86 = nc.inline_tensor(np.tile(np.arange(86, dtype=np.float32), (128, 1)), "c_iota86").ap()
    c_onesrow = nc.inline_tensor(np.ones((1, 128), BF), "c_onesrow").ap()
    c_onescol = nc.inline_tensor(np.ones((128, 1), BF), "c_onescol").ap()
    c_shift = nc.inline_tensor((np.arange(128) % 8 * 4).astype(np.uint32).reshape(128, 1),
                               "c_shift").ap()
    c_iotacol = nc.inline_tensor(np.arange(128, dtype=np.float32).reshape(128, 1),
                                 "c_iotacol").ap()
    c_eps = nc.inline_tensor(np.array([[EPS]], np.float32), "c_eps").ap()

    guW = nc.dram_tensor("guW_loc", [4, JT * 128 * 128], BF16, kind="Internal")
    guWall = nc.dram_tensor("guW_all", [32, JT * 128 * 128], BF16, kind="Internal", addr_space="Shared")
    dnW = nc.dram_tensor("dnW_loc", [11, 8 * 128 * 512], BF16, kind="Internal")
    dnWall = nc.dram_tensor("dnW_all", [88, 8 * 128 * 512], BF16, kind="Internal", addr_space="Shared")
    zg_d = nc.dram_tensor("zg_d", [32, 22016], BF16, kind="Internal")
    zd_d = nc.dram_tensor("zd_d", [96, 4096], BF16, kind="Internal")
    bar_i = nc.dram_tensor("bar_i", [1, 64], F32, kind="Internal")
    bar_o = nc.dram_tensor("bar_o", [1, 64], F32, kind="Internal")

    dbg_outs = {}
    for nm, shape, dt in dbg:
        dbg_outs[nm] = nc.dram_tensor(nm, list(shape), dt, kind="ExternalOutput").ap()

    from contextlib import ExitStack
    with ExitStack() as ctx:
        ARENA_B = 220928
        arena = ctx.enter_context(nc.sbuf_tensor("arena", [128, ARENA_B], mybir.dt.uint8))
        acast = {F32: arena.bitcast(F32), BF16: arena.bitcast(BF16), U32: arena.bitcast(U32)}

        _off = [0]

        def alloc(nbytes, align=512):
            o = (_off[0] + align - 1) // align * align
            _off[0] = o + nbytes
            return o

        def vap(off, dt, rows, dims):
            es = mybir.dt.size(dt)
            assert off % es == 0
            return bass.AP(acast[dt], off // es,
                           [[ARENA_B // es, rows]] + [list(d) for d in dims])

        def view(off, rows, cols, dt):
            return vap(off, dt, rows, [(1, cols)])

        o_X = alloc(65536)
        o_BIG = alloc(98304)
        o_ZONE = alloc(24576)
        o_pool = alloc(6144)
        o_rowz = alloc(6144)
        o_rowbf = alloc(4096)
        o_mrow = o_rowbf + 3072
        o_outb = alloc(4096)
        o_tmpf = alloc(2048)
        o_xg = alloc(1024)
        o_hg = alloc(1024)
        o_zg = alloc(1024)
        o_zd = alloc(2048)
        o_G = alloc(192)
        o_GT = alloc(256)
        o_ident = alloc(256, 64)
        o_mask = alloc(512, 64)
        o_iota32 = alloc(128, 64)
        o_iota86 = alloc(352, 64)
        o_onesrow = alloc(256, 64)
        o_onescol = alloc(256, 64)
        o_shiftv = alloc(512, 64)
        o_bT = {p: alloc(128, 64) for p in "qkvo"}
        o_gidxg = alloc(128, 64)
        o_gidrep = alloc(512, 64)
        o_gidxd = alloc(352, 64)
        o_iotacol = alloc(64, 64)
        o_eps = alloc(64, 64)
        assert _off[0] <= ARENA_B, _off[0]

        def Xt(ct, ch):
            return view(o_X + (ct * T + ch * CH) * 2, 128, CH, BF16)

        def Xcol(ct, c0, w):
            return view(o_X + (ct * T + c0) * 2, 128, w, BF16)

        s_ident = view(o_ident, 128, 128, BF16)
        s_mask = view(o_mask, 128, 128, F32)
        s_iota32 = view(o_iota32, 128, 32, F32)
        s_iota86 = view(o_iota86, 128, 86, F32)
        s_onesrow = view(o_onesrow, 1, 128, BF16)
        s_onescol = view(o_onescol, 128, 1, BF16)
        s_shift = view(o_shiftv, 128, 1, U32)
        s_bT = {p: view(o_bT[p], 128, 32, F32) for p in "qkvo"}
        s_gidxg = view(o_gidxg, 128, 32, F32)
        s_gidrep = view(o_gidrep, 96, 128, F32)
        s_gidxd = view(o_gidxd, 128, 86, F32)
        s_iotacol = view(o_iotacol, 128, 1, F32)
        s_tmpf = view(o_tmpf, 128, 512, F32)
        s_xg = view(o_xg, 32, 512, BF16)
        s_hg = view(o_hg, 86, 512, BF16)
        s_zg = [view(o_zg + i * 512, 32, 256, BF16) for i in range(2)]
        s_zd = [view(o_zd + i * 1024, 96, 512, BF16) for i in range(2)]
        s_G32 = view(o_G, 128, 32, BF16)
        s_G86 = view(o_G, 128, 86, BF16)
        s_GT = view(o_GT, 86, 128, BF16)
        s_pool = [view(o_pool + i * 2048, 128, 512, F32) for i in range(3)]
        s_row = [view(o_rowz + i * 2048, 1, 512, F32) for i in range(3)]
        s_rowbf = [view(o_rowbf + i * 1024, 1, 512, BF16) for i in range(3)]
        s_mrow = view(o_mrow, 1, 512, BF16)
        s_outb = [view(o_outb + i * 2048, 128, 512, F32) for i in range(2)]

        z = o_ZONE
        s_wf32 = [view(z + i * 2048, 128, 512, F32) for i in range(3)]
        s_wbf = [view(z + 6144 + i * 1024, 128, 512, BF16) for i in range(4)]
        o_stage = z + 10240
        s_sS = [view(z + 18432 + i * 512, 128, 128, F32) for i in range(2)]
        s_eS = [view(z + 19456 + i * 256, 128, 128, BF16) for i in range(2)]
        s_pnS = [view(z + 19968 + i * 256, 128, 128, BF16) for i in range(2)]
        s_pTS = [view(z + 20480 + i * 256, 128, 128, BF16) for i in range(2)]
        s_vbS = [view(z + 20992 + i * 256, 128, 128, BF16) for i in range(2)]
        s_rS = view(z + 21504, 128, 4, F32)
        s_guslab = [view(z + i * 8192, 128, 4096, BF16) for i in range(2)]
        s_dnslab = [view(z + 16384 + i * 1024, 128, 512, BF16) for i in range(4)]
        s_gtmp = [view(z + 20480 + i * 1024, 128, 512, BF16) for i in range(2)]
        s_sqt = [view(z + 22528 + i * 1024, 128, 512, BF16) for i in range(2)]

        wb = o_BIG + 65536
        w_qw = view(wb, 128, 2048, U32)
        w_nib = view(wb + 8192, 128, 2048, U32)
        w_sc = view(wb + 16384, 96, 2048, F32)
        w_bc = view(wb + 24576, 128, 2048, BF16)
        w_w1 = view(wb + 28672, 128, 2048, BF16)
        w_znib = view(wb, 96, 2048, U32)
        w_zf = view(wb + 8192, 96, 2048, F32)
        w_zb = view(wb + 24576, 96, 2048, BF16)
        w_scb = view(o_stage, 96, 2048, BF16)
        w_zq = view(o_stage + 4096, 96, 256, U32)
        QTB = o_BIG // 2
        KTB = o_BIG // 2 + 16384
        VTB = o_BIG // 2 + 32768

        def Hbt(ft):
            return view(o_BIG + ft * 1024, 128, 512, BF16)

        bank = [ctx.enter_context(nc.psum_tensor(f"bank{i}", [128, 512], F32))
                for i in range(8)]

        def pb(i, rows=128, cols=512, col0=0):
            return bank[i][0:rows, col0:col0 + cols]

        def pbf(i, rows=128, cols=1024, col0=0):
            return bank[i].bitcast(BF16)[0:rows, col0:col0 + cols]

        for s in ["dS", "dG", "dK", "dD", "tP", "vP", "aP", "gP", "c1", "c2", "c3",
                  "w0", "w1", "w2", "dq", "ds1", "ds2", "dgi", "z1s", "z2s", "z3s",
                  "ga", "gb", "za", "zb", "na", "nb", "nc2", "nd", "ea", "eb",
                  "oa", "ob"]:
            P.handles[s] = ctx.enter_context(nc.semaphore(s))

        def dma(dst, src, q="sync", waits=(), sem=None):
            if sem is None:
                sem = "dS" if q == "sync" else "dG"
            return P.op(q, lambda eng, d=dst, s=src: eng.dma_start(d, s),
                        waits=waits, inc=sem, inc_by=16, barrier=True)

        def mark(engine, fn, waits=()):
            sem = {"tensor": "tP", "vector": "vP", "scalar": "aP", "gpsimd": "gP"}[engine]
            return P.op(engine, fn, waits=waits, inc=sem)

        def quiet(engine, fn, waits=()):
            P.op(engine, fn, waits=waits)

        dbg_names = []

        def dump(name, src_ap, rows, cols, dt, waits=()):
            if name not in DUMP:
                return
            tt = nc.dram_tensor("dbg_" + name, [rows, cols], dt, kind="ExternalOutput")
            dbg_names.append(name)
            d = dma(bass.AP(tt, 0, [[cols, rows], [1, cols]]), src_ap,
                    waits=list(waits) + [("dD", P.v("dD"))], sem="dD")
            for e in ("tensor", "vector", "scalar", "gpsimd"):
                P.op(e, lambda eng: None, waits=[("dD", d)], barrier=True)
        build_program.dbg_names = dbg_names

        for dst, src in [(s_ident, c_ident), (s_mask, c_mask), (s_iota32, c_iota32),
                         (s_iota86, c_iota86),
                         (s_onescol, c_onescol), (s_shift, c_shift),
                         (s_gidxg, gidx_gu), (s_iotacol, c_iotacol),
                         (s_gidxd, gidx_dn)]:
            dma(dst, src, sem="dK")
        dma(s_onesrow, c_onesrow, sem="dK")
        s_eps = view(o_eps, 1, 1, F32)
        nc.const_aps.register(F32, EPS, s_eps)
        dma(s_eps, c_eps, sem="dK")
        for p in "qkvo":
            dma(s_bT[p], bTd[p], sem="dK")
        d_consts = ("dK", P.v("dK"))

        xsrc = bass.AP(xT.tensor, 0, [[T, 128], [128 * T, 32], [1, T]])
        xdst = vap(o_X, BF16, 128, [(T, 32), (1, T)])
        dma(xdst, xsrc, q="gpsimd")
        d_x = P.v("dG")

        state = {
            "wf_free": [0, 0, 0],
            "wf_sem": ["vP", "vP", "vP"],
            "wbf_free": [0, 0, 0, 0],
            "bank_free": [0] * 8,
            "bank_sem": ["vP"] * 8,
            "nconv": 0,
            "cur_pset": None,
            "z3s_total": 0,
        }

        def wbank():
            return 4 if state["cur_pset"] == 0 else 0

        def writer_gen():
            v_zq_free = 0
            v_zsc_free = 0
            d_zst = 0
            for rows, totw, qz_t, qz_w, sc_t, sc_w, zdst_t, zdst_w in [
                    (32, 2752, qz_gu.tensor, 2752, sc_gu.tensor, 22016, zg_d, 22016),
                    (96, 512, qz_dn.tensor, 512, sc_dn.tensor, 4096, zd_d, 4096)]:
                nchunk = (totw + 255) // 256
                for k in range(nchunk):
                    w0 = k * 256
                    nw = min(256, totw - w0)
                    ncol = nw * 8
                    dzq = dma(w_zq[0:rows, 0:nw],
                              bass.AP(qz_t, w0, [[qz_w, rows], [1, nw]]),
                              waits=[("vP", v_zq_free), d_consts], sem="z1s", q="sync")
                    dsc = dma(w_sc[0:rows, 0:ncol],
                              bass.AP(sc_t, w0 * 8, [[sc_w, rows], [1, ncol]]),
                              waits=[("vP", v_zsc_free)], sem="z2s", q="sync")
                    for s in range(8):
                        ap_o = bass.AP(acast[U32], wb // 4 + s,
                                       [[ARENA_B // 4, rows], [8, nw]])
                        quiet("vector", lambda eng, d=ap_o, sh=4 * s, r=rows, n=nw: eng.tensor_scalar(
                            d, w_zq[0:r, 0:n], sh, 15,
                            OP.logical_shift_right, OP.bitwise_and),
                            waits=[("z1s", dzq)] if s == 0 else [])
                    v_zq_free = P.v("vP")
                    quiet("vector", lambda eng, r=rows, n=ncol: eng.tensor_scalar(
                        w_zf[0:r, 0:n], w_znib[0:r, 0:n], 1.0, -1.0, OP.add, OP.mult))
                    v_zb = mark("vector", lambda eng, r=rows, n=ncol: eng.tensor_tensor(
                        w_zb[0:r, 0:n], w_zf[0:r, 0:n], w_sc[0:r, 0:n], OP.mult),
                        waits=[("z2s", dsc), ("z3s", d_zst)])
                    v_zsc_free = v_zb
                    d_zst = dma(bass.AP(zdst_t, w0 * 8, [[zdst_w, rows], [1, ncol]]),
                                w_zb[0:rows, 0:ncol], waits=[("vP", v_zb)],
                                sem="z3s", q="sync")
                    yield

            slices = []
            for it in range(4):
                j0 = 0
                for jw in [2048] * 10 + [1536]:
                    slices.append(("gu", it, j0, jw))
                    j0 += jw
            for kk in range(11):
                for sl in range(2):
                    slices.append(("dn", kk, sl * 2048, 2048))

            def emit_loads(idx, waits_qw, waits_sc):
                kind, blk, j0, jw = slices[idx]
                if kind == "gu":
                    qw_t, qw_rs, qw_off = qw_gu.tensor, 22016, blk * 16 * 22016 + j0
                    sc_t, sc_rs, sc_off, drows = sc_gu.tensor, 22016, j0, 32
                else:
                    qw_t, qw_rs, qw_off = qw_dn.tensor, 4096, blk * 16 * 4096 + j0
                    sc_t, sc_rs, sc_off, drows = sc_dn.tensor, 4096, j0, 96
                dq = 0
                for rpl in range(8):
                    dst = bass.AP(acast[U32], wb // 4 + rpl * (ARENA_B // 4),
                                  [[(ARENA_B // 4) * 8, 16], [1, jw]])
                    dq = dma(dst, bass.AP(qw_t, qw_off, [[qw_rs, 16], [1, jw]]),
                             waits=(list(waits_qw) if rpl == 0 else []),
                             sem="dq", q="sync")
                dsc = dma(w_sc[0:drows, 0:jw],
                          bass.AP(sc_t, sc_off, [[sc_rs, drows], [1, jw]]),
                          waits=list(waits_sc), sem="ds1", q="sync")
                return dq, dsc

            def emit_gid(idx, dma_waits, vgt_waits):
                kind, blk, _, _ = slices[idx]
                rows = 32 if kind == "gu" else 86
                src_t = gidx_guT.tensor if kind == "gu" else gidx_dnT.tensor
                dg = dma(s_gidrep[0:rows, 0:128],
                         bass.AP(src_t, blk * 128, [[0, rows], [1, 128]]),
                         waits=list(dma_waits), sem="dgi", q="sync")
                return mark("vector", lambda eng, r=rows: eng.tensor_scalar(
                    s_GT[0:r, 0:128], s_gidrep[0:r, 0:128], s_iotacol[0:r, 0:1],
                    None, OP.is_equal),
                    waits=[("dgi", dg)] + list(vgt_waits))

            vGT = emit_gid(0, [d_consts], [])
            dq, dsc = emit_loads(0, [("vP", P.v("vP"))], [("vP", P.v("vP"))])
            t_bc = 0
            a_bc = 0
            d_st = 0
            first_bc = True
            for k in range(len(slices)):
                kind, blk, j0, jw = slices[k]
                rows = 32 if kind == "gu" else 86
                a_scb = mark("scalar", lambda eng, r=rows, n=jw: eng.activation(
                    w_scb[0:r, 0:n], w_sc[0:r, 0:n], AF.Copy),
                    waits=[("ds1", dsc), ("tP", t_bc)])
                for q4 in (() if WR_NO_BC else range(jw // 512)):
                    bk = wbank()
                    t_bc = mark("tensor", lambda eng, b=bk, q=q4, r=rows: eng.matmul(
                        pb(b), s_GT[0:r, 0:128], w_scb[0:r, q * 512:(q + 1) * 512],
                        start=True, stop=True),
                        waits=[("aP", a_scb), ("vP", vGT),
                               (state["bank_sem"][bk], state["bank_free"][bk])])
                    evw = [("tP", t_bc)]
                    if first_bc:
                        evw.append(("z3s", P.v("z3s")))
                        first_bc = False
                    a_bc = mark("scalar", lambda eng, b=bk, q=q4: eng.activation(
                        w_bc[:, q * 512:(q + 1) * 512], pb(b), AF.Copy), waits=evw)
                    state["bank_free"][bk] = a_bc
                    state["bank_sem"][bk] = "aP"
                if not WR_NO_EXT:
                    v_un = mark("vector", lambda eng, n=jw: eng.tensor_scalar(
                        w_nib[:, 0:n], w_qw[:, 0:n], s_shift, 15,
                        OP.logical_shift_right, OP.bitwise_and),
                        waits=[("dq", dq)])
                    v_w1 = mark("vector", lambda eng, n=jw: eng.tensor_tensor(
                        w_w1[:, 0:n], w_nib[:, 0:n], w_bc[:, 0:n], OP.mult),
                        waits=[("ds2", d_st), ("aP", a_bc)])
                else:
                    v_un = P.v("vP")
                    v_w1 = P.v("vP")
                if kind == "gu":
                    dstap = bass.AP(guW, blk * 128 * 22016 + j0,
                                    [[22016, 128], [1, jw]])
                    srcap = view(wb + 28672, 128, jw, BF16)
                else:
                    dstap = bass.AP(dnW, blk * (8 * 65536) + (j0 // 512) * 65536,
                                    [[512, 128], [65536, jw // 512], [1, 512]])
                    srcap = vap(wb + 28672, BF16, 128, [(512, jw // 512), (1, 512)])
                if not WR_NO_STORE:
                    d_st = dma(dstap, srcap, waits=[("vP", v_w1)], sem="ds2", q="sync")
                if k + 1 < len(slices):
                    nkind, nblk = slices[k + 1][0], slices[k + 1][1]
                    if (nkind, nblk) != (kind, blk) and not WR_NO_GID:
                        vGT = emit_gid(k + 1, [("vP", vGT)], [("tP", t_bc)])
                    dq, dsc = emit_loads(k + 1, [("vP", v_un)], [("aP", a_scb)])
                yield

        w_gen = writer_gen()
        wr_cnt = [0]

        def pump(n=4):
            for _ in range(n):
                if wr_cnt[0] >= WR_LIMIT:
                    break
                if next(w_gen, "END") == "END":
                    break
                wr_cnt[0] += 1

        def writer_drain():
            while wr_cnt[0] < WR_LIMIT:
                if next(w_gen, "END") == "END":
                    break
                wr_cnt[0] += 1
            state["z3s_total"] = P.v("z3s")
            wrv = [(nm, P.v(nm)) for nm in
                   ("dq", "ds1", "ds2", "z2s", "z3s")] + [("aP", P.v("aP"))]
            P.op("gpsimd", lambda eng: eng.collective_compute(
                "AllGather", OP.bypass, replica_groups=[list(range(8))],
                ins=[guW.ap().opt()], outs=[guWall.ap().opt()]),
                waits=[("ds2", P.v("ds2"))], inc="c1")
            P.op("gpsimd", lambda eng: eng.collective_compute(
                "AllGather", OP.bypass, replica_groups=[list(range(8))],
                ins=[dnW.ap().opt()], outs=[dnWall.ap().opt()]), inc="c2")
            P.op("gpsimd", lambda eng: eng.collective_compute(
                "AllReduce", OP.add, replica_groups=[list(range(8))],
                ins=[bar_i.ap().opt()], outs=[bar_o.ap().opt()]), inc="c3")
            dump("guw", bass.AP(guW.bitcast(F32), 0, [[JT * 8192, 4], [1, JT * 8192]]),
                 4, JT * 8192, F32, waits=list(wrv))
            dump("dnw", bass.AP(dnW.bitcast(F32), 0, [[8 * 32768, 11], [1, 8 * 32768]]),
                 11, 8 * 32768, F32, waits=list(wrv))
            dump("zg", bass.AP(zg_d.bitcast(F32), 0, [[11008, 32], [1, 11008]]),
                 32, 11008, F32, waits=list(wrv))
            dump("zd", bass.AP(zd_d.bitcast(F32), 0, [[2048, 86], [1, 2048]]),
                 86, 2048, F32, waits=list(wrv))
            return wrv

        def ln_phase(ch, which, extra_first=()):
            first_w = list(extra_first) + [d_consts, ("dG", d_x)]
            t_s1 = t_s2 = 0
            a_sq = 0
            for ct in range(32):
                a_sq = mark("scalar", lambda eng, c=ct: eng.activation(
                    s_sqt[c % 2], Xt(c, ch), AF.Square),
                    waits=(first_w if ct == 0 else [("tP", t_s2)]))
                w0 = []
                if ct == 0:
                    w0 = first_w + [(state["bank_sem"][i], state["bank_free"][i])
                                    for i in (0, 1, 2)]
                t_s1 = mark("tensor", lambda eng, c=ct: eng.matmul(
                    pb(0, 1, 512), s_onescol, Xt(c, ch),
                    start=(c == 0), stop=(c == 31)), waits=w0)
                t_s2 = mark("tensor", lambda eng, c=ct: eng.matmul(
                    pb(1, 1, 512), s_onescol, s_sqt[c % 2],
                    start=(c == 0), stop=(c == 31)), waits=[("aP", a_sq)])
            quiet("vector", lambda eng: eng.tensor_scalar(
                s_row[0], pb(0, 1, 512), 1.0 / HID, None, OP.mult),
                waits=[("tP", t_s1)])
            quiet("vector", lambda eng: eng.tensor_copy(s_mrow, s_row[0]))
            quiet("vector", lambda eng: eng.tensor_scalar(
                s_row[1], pb(1, 1, 512), 1.0 / HID, None, OP.mult),
                waits=[("tP", t_s2)])
            quiet("vector", lambda eng: eng.tensor_tensor(
                s_row[2], s_row[0], s_row[0], OP.mult))
            v_var = mark("vector", lambda eng: eng.tensor_tensor(
                s_row[1], s_row[1], s_row[2], OP.subtract))
            a_sqr = mark("scalar", lambda eng: eng.activation(
                s_row[2], s_row[1], AF.Sqrt, bias=EPS), waits=[("vP", v_var)])
            quiet("vector", lambda eng: eng.tensor_copy(s_rowbf[0], s_row[2]),
                  waits=[("aP", a_sqr)])
            quiet("vector", lambda eng: eng.reciprocal(s_row[1], s_row[2]))
            quiet("vector", lambda eng: eng.tensor_copy(s_rowbf[1], s_row[1]))
            quiet("vector", lambda eng: eng.scalar_tensor_tensor(
                s_row[2], s_row[0], -1.0, s_row[1], OP.mult, OP.mult))
            v_rows = mark("vector", lambda eng: eng.tensor_copy(s_rowbf[2], s_row[2]))
            v_bc = 0
            for slot, rb in ((0, 1), (1, 2), (2, 0)):
                t_b = mark("tensor", lambda eng, r=rb: eng.matmul(
                    pb(2), s_onesrow, s_rowbf[r], start=True, stop=True),
                    waits=[("vP", max(v_rows, v_bc))])
                v_bc = mark("vector", lambda eng, sl=slot: eng.tensor_copy(
                    s_pool[sl], pb(2)), waits=[("tP", t_b)])
            for i in (0, 1, 2):
                state["bank_free"][i] = v_bc
                state["bank_sem"][i] = "vP"
            v_tr = 0
            for ct in range(32):
                quiet("vector", lambda eng, c=ct: eng.tensor_tensor(
                    s_tmpf, Xt(c, ch), s_pool[0], OP.mult),
                    waits=[("vP", v_bc)] if ct == 0 else [])
                v_tr = mark("vector", lambda eng, c=ct: eng.tensor_tensor(
                    Xt(c, ch), s_tmpf, s_pool[1], OP.add))
            return v_tr

        def proj_phase(ch, p, rhs_base, dst_base, extra_mm_wait=(), extra_evac_wait=(),
                       fold_mrow=False, recon=False, pump_fn=None):
            v_ev = 0
            for og in range(8):
                pset = (og % 2) * 4
                state["cur_pset"] = pset
                if pump_fn is not None:
                    pump_fn(4)
                tmm = 0
                for ct in range(32):
                    n = state["nconv"]
                    i3 = n % 3
                    j4 = n % 4
                    state["nconv"] = n + 1
                    if PROJ_SERIAL == "tP":
                        ser = [("tP", P.v("tP"))]
                    elif PROJ_SERIAL == "vP":
                        ser = [("vP", P.v("vP"))]
                    elif PROJ_SERIAL:
                        ser = [("tP", P.v("tP")), ("vP", P.v("vP"))]
                    else:
                        ser = []
                    dW = dma(s_wf32[i3],
                             bass.AP(WTd[p].tensor, ct * 128 * HID + og * 512,
                                     [[HID, 128], [1, 512]]),
                             waits=[(state["wf_sem"][i3], state["wf_free"][i3])] + ser,
                             sem=f"w{i3}")
                    ceng = "gpsimd" if (GP_CONVERT and n % 2 == 0) else "vector"
                    cm = mark(ceng, lambda eng, a=j4, b=i3: eng.tensor_copy(
                        s_wbf[a], s_wf32[b]),
                        waits=[(f"w{i3}", dW), ("tP", state["wbf_free"][j4])])
                    state["wf_free"][i3] = cm
                    state["wf_sem"][i3] = "gP" if ceng == "gpsimd" else "vP"
                    csem = state["wf_sem"][i3]
                    if rhs_base is None:
                        rhs = Xt(ct, ch)
                    else:
                        rhs = bass.AP(acast[BF16], rhs_base + ct * 128,
                                      [[ARENA_B // 2, 128], [4096, 4], [1, 128]])
                    for ot in range(4):
                        bk = pset + ot
                        w = [(csem, cm)] if ot == 0 else []
                        if ct == 0:
                            w.append((state["bank_sem"][bk], state["bank_free"][bk]))
                            w += list(extra_mm_wait)
                        tmm = mark("tensor", lambda eng, b=bk, j=j4, o=ot, r=rhs, c=ct: eng.matmul(
                            pb(b), s_wbf[j][:, o * 128:(o + 1) * 128], r,
                            start=(c == 0), stop=(c == 31 and not fold_mrow)), waits=w)
                    state["wbf_free"][j4] = tmm
                if fold_mrow:
                    for ot in range(4):
                        tmm = mark("tensor", lambda eng, b=pset + ot: eng.matmul(
                            pb(b), s_onesrow, s_mrow, start=False, stop=True))
                for ot in range(4):
                    bk = pset + ot
                    o = og * 4 + ot
                    ew = list(extra_evac_wait) if (og == 0 and ot == 0) else []
                    if not recon:
                        dstap = bass.AP(acast[BF16], dst_base + o,
                                        [[ARENA_B // 2, 128], [128, 128], [32, 4]])
                        srcap = bass.AP(bank[bk], 0, [[512, 128], [4, 128], [1, 4]])
                        v_ev = mark("vector", lambda eng, sp=srcap, oo=o, d=dstap: eng.tensor_scalar(
                            d, sp, s_bT[p][:, oo:oo + 1], None, OP.add),
                            waits=[("tP", tmm)] + ew)
                    else:
                        quiet("vector", lambda eng, oo=o: eng.tensor_tensor(
                            s_tmpf, Xt(oo, ch), s_pool[2], OP.mult),
                            waits=[("tP", tmm)] + ew)
                        v_ev = mark("vector", lambda eng, b=bk, oo=o: eng.scalar_tensor_tensor(
                            Xt(oo, ch), pb(b), s_bT[p][:, oo:oo + 1], s_tmpf,
                            OP.add, OP.add))
                    state["bank_free"][bk] = v_ev
                    state["bank_sem"][bk] = "vP"
                if og == 0 and p == "q" and ch == 0:
                    wv = [("vP", P.v("vP"))]
                    dump("qog0", view(o_BIG, 128, 8192, F32), 128, 8192,
                         F32, waits=wv)
                    dump("zwf", view(o_ZONE, 128, 1536, F32), 128, 1536,
                         F32, waits=wv)
                    dump("zwb", view(o_ZONE + 6144, 128, 1024, F32), 128, 1024,
                         F32, waits=wv)
            return v_ev

        def attn_core(ch, v_qkv_done, extra_first=()):
            t_sc = {}
            t_pt = {}
            t_vb = {}
            t_av = {}
            m_sstt = {}
            m_exp = {}
            m_v0 = {}
            m_v1 = {}
            m_v2 = {}
            m_v3 = {}
            v_deint = 0
            for G in range(4):
                g0 = G * 32
                for b in range(33):
                    gb = g0 + b
                    if b < 32:
                        r = gb % 2
                        qb = bass.AP(acast[BF16], QTB + gb * 128,
                                     [[ARENA_B // 2, 128], [1, 128]])
                        kb = bass.AP(acast[BF16], KTB + gb * 128,
                                     [[ARENA_B // 2, 128], [1, 128]])
                        vbp = bass.AP(acast[BF16], VTB + gb * 128,
                                      [[ARENA_B // 2, 128], [1, 128]])
                        w = [("vP", m_sstt.get(gb - 2, 0))]
                        if gb == 0:
                            w += [("vP", v_qkv_done)] + list(extra_first)
                        t_sc[gb] = mark("tensor", lambda eng, q=qb, k=kb, r2=r: eng.matmul(
                            pb(0 + r2, 128, 128), q, k, start=True, stop=True), waits=w)
                        t_vb[gb] = mark("tensor", lambda eng, v=vbp, r2=r: eng.transpose(
                            pbf(4 + r2, 128, 128), v, s_ident),
                            waits=[("vP", m_v2.get(gb - 2, 0))])
                    if b >= 1:
                        gbb = g0 + b - 1
                        r2 = gbb % 2
                        t_pt[gbb] = mark("tensor", lambda eng, r3=r2: eng.transpose(
                            pbf(2 + r3, 128, 128), s_eS[r3], s_ident),
                            waits=[("vP", m_v0[gbb]), ("vP", m_v1.get(gbb - 2, 0))])
                        m_v1[gbb] = mark("vector", lambda eng, r3=r2: eng.tensor_copy(
                            s_pTS[r3], pbf(2 + r3, 128, 128)),
                            waits=[("tP", t_pt[gbb]), ("tP", t_av.get(gbb - 2, 0))])
                        m_v2[gbb] = mark("vector", lambda eng, r3=r2: eng.tensor_copy(
                            s_vbS[r3], pbf(4 + r3, 128, 128)),
                            waits=[("tP", t_vb[gbb])])
                        t_av[gbb] = mark("tensor", lambda eng, r3=r2: eng.matmul(
                            pb(6 + r3, 128, 128), s_vbS[r3], s_pTS[r3],
                            start=True, stop=True),
                            waits=[("vP", m_v2[gbb]),
                                   ("vP", m_v3.get(gbb - 2, 0))])
                    if b < 32:
                        r2 = gb % 2
                        m_sstt[gb] = mark("vector", lambda eng, r3=r2: eng.scalar_tensor_tensor(
                            s_sS[r3], pb(0 + r3, 128, 128), ISD, s_mask, OP.mult, OP.add),
                            waits=[("tP", t_sc[gb]), ("aP", m_exp.get(gb - 2, 0))])
                        m_exp[gb] = mark("scalar", lambda eng, r3=r2: eng.activation(
                            s_eS[r3], s_sS[r3], AF.Exp),
                            waits=[("vP", m_sstt[gb]),
                                   ("tP", t_pt.get(gb - 2, 0))])
                        quiet("vector", lambda eng, r3=r2: eng.scalar_tensor_tensor(
                            s_pnS[r3], s_sS[r3], THRESH, s_eS[r3], OP.is_gt, OP.mult),
                            waits=[("aP", m_exp[gb])])
                        quiet("vector", lambda eng, r3=r2: eng.reduce_sum(
                            s_rS[:, r3:r3 + 1], s_pnS[r3], axis=AX.X))
                        quiet("vector", lambda eng, r3=r2: eng.reciprocal(
                            s_rS[:, 2 + r3:3 + r3], s_rS[:, r3:r3 + 1]))
                        m_v0[gb] = mark("vector", lambda eng, r3=r2: eng.tensor_scalar(
                            s_eS[r3], s_pnS[r3], s_rS[:, 2 + r3:3 + r3], None, OP.mult))
                    if b >= 1:
                        gbb = g0 + b - 1
                        r2 = gbb % 2
                        m_v3[gbb] = mark("vector", lambda eng, r3=r2, b2=b - 1: eng.tensor_copy(
                            vap(o_stage + b2 * 256, BF16, 128, [(1, 128)]),
                            pb(6 + r3, 128, 128)),
                            waits=[("tP", t_av[gbb])])
                for h in range(32):
                    src = bass.AP(acast[BF16], o_stage // 2 + h,
                                  [[ARENA_B // 2, 128], [128, 32], [32, 4]])
                    dst = bass.AP(acast[BF16], KTB + G * 4096 + h * 128,
                                  [[ARENA_B // 2, 128], [4, 32], [1, 4]])
                    hw = [("tP", t_sc[g0 + 31]), ("vP", m_v3[g0 + 31])] if h == 0 else []
                    if h == 31:
                        v_deint = mark("vector",
                                       lambda eng, s2=src, d2=dst: eng.tensor_copy(d2, s2),
                                       waits=hw)
                    else:
                        quiet("vector", lambda eng, s2=src, d2=dst: eng.tensor_copy(d2, s2),
                              waits=hw)
            for i in range(8):
                state["bank_free"][i] = P.v("vP")
                state["bank_sem"][i] = "vP"
            return v_deint

        def ffn_phase(ch, v_ln2_done, first_gu_waits):
            t_x = 0
            for ct in range(32):
                vg = mark("vector", lambda eng, c=ct: eng.tensor_scalar(
                    s_G32, s_iota32, s_gidxg[:, c:c + 1], None, OP.is_equal),
                    waits=[("tP", t_x), ("vP", v_ln2_done if ct == 0 else 0)])
                wx = [("vP", vg)]
                if ct == 0:
                    wx.append((state["bank_sem"][4], state["bank_free"][4]))
                t_x = mark("tensor", lambda eng, c=ct: eng.matmul(
                    pb(4, 32, 512), s_G32, Xt(c, ch), start=(c == 0), stop=(c == 31)),
                    waits=wx)
            v_xg = mark("vector", lambda eng: eng.tensor_copy(s_xg, pb(4, 32, 512)),
                        waits=[("tP", t_x)])
            slab_free = [0, 0]
            gate_free = [0, 0]
            up_free = [0, 0]
            zg_free = [0, 0]
            a_silu = {}
            v_h = {}
            d_zg = {}
            for idx in range(2 * FT):
                ft = idx // 2
                is_up = idx % 2
                jt = ft + 86 * is_up
                sl = idx % 2
                w = [("tP", slab_free[sl])]
                if idx == 0:
                    w += list(first_gu_waits) + [("c1", 1), ("c3", 1)]
                src = bass.AP(guWall, jt * 128,
                              [[22016, 128], [JT * 16384, 32], [1, 128]])
                gsem = "ga" if sl == 0 else "gb"
                dsl = dma(vap(z + sl * 8192, BF16, 128, [(128, 32), (1, 128)]), src,
                          waits=w, sem=gsem)
                if not is_up:
                    zsem = "za" if ft % 2 == 0 else "zb"
                    zw = [("tP", zg_free[ft % 2])]
                    if ft < 2:
                        zw.append(("z3s", state["z3s_total"]))
                    z1 = dma(s_zg[ft % 2][0:32, 0:128],
                             bass.AP(zg_d, ft * 128, [[22016, 32], [1, 128]]),
                             waits=zw, sem=zsem, q="sync")
                    z2 = dma(s_zg[ft % 2][0:32, 128:256],
                             bass.AP(zg_d, (86 + ft) * 128, [[22016, 32], [1, 128]]),
                             sem=zsem, q="sync")
                    d_zg[ft] = (zsem, z2)
                bk = (2 if is_up else 0) + ft % 2
                wmm0 = [(gsem, dsl)]
                if not is_up:
                    wmm0.append(("aP", gate_free[ft % 2]))
                else:
                    wmm0.append(("vP", up_free[ft % 2]))
                tmm = 0
                for ct in range(32):
                    tmm = mark("tensor", lambda eng, b=bk, c=ct, s2=sl: eng.matmul(
                        pb(b), s_guslab[s2][:, c * 128:(c + 1) * 128], Xt(c, ch),
                        start=(c == 0), stop=False),
                        waits=(wmm0 if ct == 0 else []))
                tz = mark("tensor", lambda eng, b=bk, f=ft, u=is_up: eng.matmul(
                    pb(b), s_zg[f % 2][0:32, u * 128:(u + 1) * 128], s_xg,
                    start=False, stop=True),
                    waits=[d_zg[ft], ("vP", v_xg)])
                slab_free[sl] = tz
                if not is_up:
                    a_silu[ft] = mark("scalar", lambda eng, b=bk, f=ft: eng.activation(
                        s_gtmp[f % 2], pb(b), AF.Silu),
                        waits=[("tP", tz), ("vP", v_h.get(ft - 2, 0))])
                    gate_free[ft % 2] = a_silu[ft]
                else:
                    v_h[ft] = mark("vector", lambda eng, b=bk, f=ft: eng.tensor_tensor(
                        Hbt(f), s_gtmp[f % 2], pb(b), OP.mult),
                        waits=[("tP", tz), ("aP", a_silu[ft])])
                    up_free[ft % 2] = v_h[ft]
                    zg_free[ft % 2] = tz
            state["bank_free"][0] = a_silu[84]; state["bank_sem"][0] = "aP"
            state["bank_free"][1] = a_silu[85]; state["bank_sem"][1] = "aP"
            state["bank_free"][2] = v_h[84]; state["bank_sem"][2] = "vP"
            state["bank_free"][3] = v_h[85]; state["bank_sem"][3] = "vP"
            t_hg = 0
            for ft in range(FT):
                vg = mark("vector", lambda eng, c=ft: eng.tensor_scalar(
                    s_G86, s_iota86, s_gidxd[:, c:c + 1], None, OP.is_equal),
                    waits=[("tP", t_hg)])
                t_hg = mark("tensor", lambda eng, c=ft: eng.matmul(
                    pb(4, 86, 512), s_G86, Hbt(c), start=(c == 0), stop=(c == 85)),
                    waits=[("vP", max(vg, v_h[ft]))] +
                          ([("vP", v_xg)] if ft == 0 else []))
            v_hg = mark("vector", lambda eng: eng.tensor_copy(s_hg, pb(4, 86, 512)),
                        waits=[("tP", t_hg)])
            state["bank_free"][4] = v_hg
            state["bank_sem"][4] = "vP"
            dn_free = [0, 0, 0, 0]
            zd_free = [0, 0]
            v_ev = 0
            for og in range(8):
                pset = (og % 2) * 4
                esem = "ea" if og % 2 == 0 else "eb"
                zdw = [("tP", zd_free[og % 2])]
                if og < 2:
                    zdw.append(("z3s", state["z3s_total"]))
                d_zd = dma(s_zd[og % 2],
                           bass.AP(zd_d, og * 512, [[4096, 96], [1, 512]]),
                           waits=zdw, sem=esem, q="sync")
                for ot in range(4):
                    bk = pset + ot
                    mark("tensor", lambda eng, b=bk: eng.matmul(
                        pb(b), s_onesrow, s_mrow, start=True, stop=False),
                        waits=[(state["bank_sem"][bk], state["bank_free"][bk])] +
                              ([("vP", v_hg)] if og == 0 and ot == 0 else []))
                tmm = 0
                for ft in range(FT):
                    k4 = (og * FT + ft) % 4
                    nsem = ("na", "nb", "nc2", "nd")[k4]
                    dsl = dma(s_dnslab[k4],
                              bass.AP(dnWall, ft * (8 * 65536) + og * 65536,
                                      [[512, 128], [1, 512]]),
                              waits=[("tP", dn_free[k4])] +
                                    ([("c2", 1)] if og == 0 and ft == 0 else []),
                              sem=nsem, q="sync")
                    for ot in range(4):
                        bk = pset + ot
                        tmm = mark("tensor", lambda eng, b=bk, k=k4, o=ot, f=ft: eng.matmul(
                            pb(b), s_dnslab[k][:, o * 128:(o + 1) * 128], Hbt(f),
                            start=False, stop=False),
                            waits=[(nsem, dsl)] if ot == 0 else [])
                    dn_free[k4] = tmm
                tz = 0
                for ot in range(4):
                    bk = pset + ot
                    tz = mark("tensor", lambda eng, b=bk, o=ot, g=og: eng.matmul(
                        pb(b), s_zd[g % 2][0:86, o * 128:(o + 1) * 128], s_hg,
                        start=False, stop=True),
                        waits=[(esem, d_zd), ("vP", v_hg)] if ot == 0 else [])
                zd_free[og % 2] = tz
                for ot in range(4):
                    bk = pset + ot
                    o = og * 4 + ot
                    quiet("vector", lambda eng, oo=o: eng.tensor_tensor(
                        s_tmpf, Xt(oo, ch), s_pool[2], OP.mult), waits=[("tP", tz)])
                    v_ev = mark("vector", lambda eng, b=bk, oo=o: eng.tensor_tensor(
                        Xt(oo, ch), pb(b), s_tmpf, OP.add))
                    state["bank_free"][bk] = v_ev
                    state["bank_sem"][bk] = "vP"
            return v_ev

        def out_phase(ch, v_x2_done):
            st_free = [0, 0]
            for tt in range(4):
                for g4 in range(8):
                    r = g4 % 2
                    t_tr = 0
                    for ci in range(4):
                        ct = g4 * 4 + ci
                        w = []
                        if ci == 0:
                            w = [("vP", v_x2_done),
                                 (state["bank_sem"][r], state["bank_free"][r])]
                        t_tr = mark("tensor", lambda eng, c=ct, i=ci, r2=r,
                                    c0=ch * CH + tt * 128: eng.transpose(
                            pbf(r2, 128, 128, col0=i * 128),
                            Xcol(c, c0, 128), s_ident), waits=w)
                    osem = "oa" if r == 0 else "ob"
                    v_e = mark("vector", lambda eng, r2=r: eng.tensor_copy(
                        s_outb[r2], pbf(r2, 128, 512)),
                        waits=[("tP", t_tr), (osem, st_free[r])])
                    state["bank_free"][r] = v_e
                    state["bank_sem"][r] = "vP"
                    dst = bass.AP(out.tensor, (ch * CH + tt * 128) * HID + g4 * 512,
                                  [[HID, 128], [1, 512]])
                    P.op("sync", lambda eng, d=dst, r2=r: eng.dma_start(d, s_outb[r2]),
                         waits=[("vP", v_e)], inc=osem, inc_by=16)
                    st_free[r] = P.v(osem)

        prev = {"t_dn_last": 0}
        for ch in range(2):
            if ch == 0:
                evac_gate = []
                ln1_first = []
            else:
                evac_gate = [("tP", prev["t_dn_last"])]
                ln1_first = [("tP", prev["t_dn_last"])]
            v_ln1 = ln_phase(ch, 1, extra_first=ln1_first)
            if ch == 0:
                state["cur_pset"] = None
                pump(4)
                dump("ln1", view(o_X, 128, 16384, F32), 128, 16384, F32,
                     waits=[("vP", v_ln1)])
                dump("rows1", view(o_rowz, 1, 1536, F32), 1, 1536, F32,
                     waits=[("vP", v_ln1)])
                dump("pool1", view(o_pool, 128, 1536, F32), 128, 1536, F32,
                     waits=[("vP", v_ln1)])
            v_q = proj_phase(ch, "q", None, QTB, extra_mm_wait=[("vP", v_ln1)],
                             extra_evac_wait=evac_gate,
                             pump_fn=pump if ch == 0 else None)
            if ch == 0:
                dump("qbuf", view(o_BIG, 128, 8192, F32), 128, 8192, F32,
                     waits=[("vP", v_q)])
                dump("qbuf2", view(o_X, 128, 16384, F32), 128, 16384, F32,
                     waits=[("vP", v_q)])
            v_k = proj_phase(ch, "k", None, KTB,
                             pump_fn=pump if ch == 0 else None)
            if ch == 0:
                v_gate = writer_drain()
            else:
                v_gate = []
            v_v = proj_phase(ch, "v", None, VTB, extra_evac_wait=v_gate)
            if ch == 0:
                dump("kbuf", view(o_BIG + 32768, 128, 8192, F32), 128, 8192,
                     F32, waits=[("vP", v_v)])
                dump("vbuf", view(o_BIG + 65536, 128, 8192, F32), 128, 8192,
                     F32, waits=[("vP", v_v)])
            core_extra = [("tP", prev["t_dn_last"])] if ch == 1 else []
            v_deint = attn_core(ch, v_v, extra_first=core_extra)
            if ch == 0:
                dump("abuf", view(o_BIG + 32768, 128, 8192, F32), 128, 8192,
                     F32, waits=[("vP", v_deint)])
            v_wo = proj_phase(ch, "o", KTB, None, fold_mrow=True, recon=True,
                              extra_mm_wait=[("vP", v_deint)])
            if ch == 0:
                dump("x1", view(o_X, 128, 16384, F32), 128, 16384, F32,
                     waits=[("vP", v_wo)])
            v_ln2 = ln_phase(ch, 2, extra_first=[("vP", v_wo)])
            if ch == 0:
                dump("ln2", view(o_X, 128, 16384, F32), 128, 16384, F32,
                     waits=[("vP", v_ln2)])
                dump("rows2", view(o_rowz, 1, 1536, F32), 1, 1536, F32,
                     waits=[("vP", v_ln2)])
            first_gu = [("vP", v_deint), ("tP", P.v("tP"))]
            v_x2 = ffn_phase(ch, v_ln2, first_gu)
            if ch == 0:
                dump("xgbuf", view(o_xg, 32, 256, F32), 32, 256, F32,
                     waits=[("vP", v_x2)])
                dump("hgbuf", view(o_hg, 86, 256, F32), 86, 256, F32,
                     waits=[("vP", v_x2)])
                dump("hbuf", view(o_BIG, 128, 86 * 256, F32), 128, 86 * 256,
                     F32, waits=[("vP", v_x2)])
                dump("x2", view(o_X, 128, 16384, F32), 128, 16384, F32,
                     waits=[("vP", v_x2)])
            prev["t_dn_last"] = P.v("tP")
            out_phase(ch, v_x2)

        P.op("sync", lambda eng: None,
             waits=[("oa", P.v("oa")), ("ob", P.v("ob"))])

        for nm in dbg_outs:
            pass

        with nc.Block() as block:
            @block.sync
            def _(eng):
                for e, f in P.ops:
                    if e == "sync":
                        f(eng)

            @block.tensor
            def _(eng):
                for e, f in P.ops:
                    if e == "tensor":
                        f(eng)

            @block.vector
            def _(eng):
                for e, f in P.ops:
                    if e == "vector":
                        f(eng)

            @block.scalar
            def _(eng):
                for e, f in P.ops:
                    if e == "scalar":
                        f(eng)

            @block.gpsimd
            def _(eng):
                for e, f in P.ops:
                    if e == "gpsimd":
                        f(eng)

    return nc


def kernel(**inputs):
    f32 = np.float32
    x = np.asarray(inputs["L0_attn_x"], f32).reshape(8192, HID)
    xT_full = np.ascontiguousarray(x.T)
    WT = {p: np.ascontiguousarray(np.asarray(inputs[f"L0_attn_W{p}"], f32).T)
          for p in "qkvo"}
    bT = {p: np.ascontiguousarray(np.asarray(inputs[f"L0_attn_b{p}"], f32)
                                  .reshape(32, 128).T) for p in "qkvo"}
    qwg = np.asarray(inputs["L0_ffn_qweight_gu"], np.uint32)
    qzg = np.asarray(inputs["L0_ffn_qzeros_gu"], np.uint32)
    scg = np.asarray(inputs["L0_ffn_scales_gu"], f32)
    gig = np.asarray(inputs["L0_ffn_g_idx_gu"]).astype(f32)
    qwd = np.asarray(inputs["L0_ffn_qweight_down"], np.uint32)
    qzd = np.asarray(inputs["L0_ffn_qzeros_down"], np.uint32)
    scd = np.asarray(inputs["L0_ffn_scales_down"], f32)
    gid = np.asarray(inputs["L0_ffn_g_idx_down"]).astype(f32)

    gidxg = np.ascontiguousarray(gig.reshape(32, 128).T)
    gidxd = np.ascontiguousarray(gid.reshape(86, 128).T)
    qwd_pad = np.zeros((8 * 176, 4096), np.uint32)
    qwd_pad[:1376] = qwd
    qzd_pad = np.zeros((96, 512), np.uint32)
    qzd_pad[:86] = qzd
    scd_pad = np.zeros((96, 4096), f32)
    scd_pad[:86] = scd
    gigT = gig.reshape(32, 128)
    gidT_pad = np.zeros((88, 128), f32)
    gidT_pad[:86] = gid.reshape(86, 128)

    nc = build_program()
    in_maps = []
    for c in range(8):
        m = dict(
            xT=np.ascontiguousarray(xT_full[:, c * T:(c + 1) * T]),
            qw_gu=np.ascontiguousarray(qwg[c * 64:(c + 1) * 64]),
            qz_gu=qzg, sc_gu=scg,
            gidx_gu=gidxg,
            gidx_guT=np.ascontiguousarray(gigT[c * 4:(c + 1) * 4]),
            qw_dn=np.ascontiguousarray(qwd_pad[c * 176:(c + 1) * 176]),
            qz_dn=qzd_pad, sc_dn=scd_pad,
            gidx_dn=gidxd,
            gidx_dnT=np.ascontiguousarray(gidT_pad[c * 11:(c + 1) * 11]),
        )
        for p in "qkvo":
            m[f"W{p}T"] = WT[p]
            m[f"b{p}T"] = bT[p]
        in_maps.append(m)
    res = run_bass_kernel_spmd(nc, in_maps, core_ids=list(range(8)))
    if DUMP:
        DBG_RESULTS.clear()
        for c in range(8):
            DBG_RESULTS.append({n: np.asarray(res.results[c]["dbg_" + n])
                                for n in build_program.dbg_names})
    parts = [np.asarray(res.results[c]["out"]) for c in range(8)]
    return np.concatenate(parts, axis=0).reshape(4, 2048, HID).astype(np.float32)


# version 12 (final)
# speedup vs baseline: 1.0088x; 1.0028x over previous
import math
import numpy as np
import ml_dtypes
import concourse.bass as bass
import concourse.mybir as mybir
from concourse.bass_utils import run_bass_kernel_spmd

F32 = mybir.dt.float32
BF16 = mybir.dt.bfloat16
U32 = mybir.dt.uint32
AF = mybir.ActivationFunctionType
OP = mybir.AluOpType
AX = mybir.AxisListType
BF = ml_dtypes.bfloat16

DUMP = set()
WR_LIMIT = 10**9
WR_NO_BC = False
WR_NO_EXT = False
WR_NO_STORE = False
WR_NO_GID = False
DBG_RESULTS = []

T = 1024
CH = 512
HID = 4096
JT = 172
FT = 86
ISD = 1.0 / math.sqrt(128.0)
THRESH = 0.001
EPS = 1e-5
GP_CONVERT = False
PROJ_SERIAL = False


ESEM = {"tensor": "tP", "vector": "vP", "scalar": "aP", "gpsimd": "gP"}


class Prog:

    def __init__(self):
        self.ops = []
        self.cnt = {}
        self.handles = {}

    def v(self, name):
        return self.cnt.get(name, 0)

    def op(self, engine, fn, waits=(), inc=None, inc_by=1, barrier=False):
        waits = list(waits)
        if engine in ("vector", "scalar", "gpsimd") and not barrier:
            sem = ESEM[engine]
            prev = self.v(sem)
            if prev > 0:
                waits.append((sem, prev))
            if inc is None:
                inc, inc_by = sem, 1
        waits = [(s, v) for (s, v) in waits if v > 0]
        tgt = None
        if inc is not None:
            self.cnt[inc] = self.cnt.get(inc, 0) + inc_by
            tgt = (inc, inc_by)

        def emit(eng, _fn=fn, _w=waits, _t=tgt, _P=self):
            for s, vv in _w:
                eng.wait_ge(_P.handles[s], vv)
            inst = _fn(eng)
            if _t is not None:
                inst.then_inc(_P.handles[_t[0]], _t[1])
        self.ops.append((engine, emit))
        return self.v(inc) if inc else None


def build_program(dbg=()):
    nc = bass.Bass("TRN2", target_bir_lowering=False, debug=False,
                   dynamic_dma_scratch_size=8192)
    P = Prog()

    def inp(name, shape, dt):
        return nc.dram_tensor(name, list(shape), dt, kind="ExternalInput").ap()

    xT = inp("xT", (HID, T), F32)
    WTd = {p: inp(f"W{p}T", (HID, HID), F32) for p in "qkvo"}
    bTd = {p: inp(f"b{p}T", (128, 32), F32) for p in "qkvo"}
    qw_gu = inp("qw_gu", (64, 22016), U32)
    qz_gu = inp("qz_gu", (32, 2752), U32)
    sc_gu = inp("sc_gu", (32, 22016), F32)
    gidx_gu = inp("gidx_gu", (128, 32), F32)
    gidx_guT = inp("gidx_guT", (4, 128), F32)
    qw_dn = inp("qw_dn", (176, 4096), U32)
    qz_dn = inp("qz_dn", (96, 512), U32)
    sc_dn = inp("sc_dn", (96, 4096), F32)
    gidx_dn = inp("gidx_dn", (128, 86), F32)
    gidx_dnT = inp("gidx_dnT", (11, 128), F32)

    out = nc.dram_tensor("out", [T, HID], F32, kind="ExternalOutput").ap()

    mask_np = np.full((128, 128), -1e30, np.float32)
    for b in range(4):
        mask_np[b * 32:(b + 1) * 32, b * 32:(b + 1) * 32] = 0.0
    c_ident = nc.inline_tensor(np.eye(128, dtype=BF), "c_ident").ap()
    c_mask = nc.inline_tensor(mask_np, "c_mask").ap()
    c_iota32 = nc.inline_tensor(np.tile(np.arange(32, dtype=np.float32), (128, 1)), "c_iota32").ap()
    c_iota86 = nc.inline_tensor(np.tile(np.arange(86, dtype=np.float32), (128, 1)), "c_iota86").ap()
    c_onesrow = nc.inline_tensor(np.ones((1, 128), BF), "c_onesrow").ap()
    c_onescol = nc.inline_tensor(np.ones((128, 1), BF), "c_onescol").ap()
    c_shift = nc.inline_tensor((np.arange(128) % 8 * 4).astype(np.uint32).reshape(128, 1),
                               "c_shift").ap()
    c_iotacol = nc.inline_tensor(np.arange(128, dtype=np.float32).reshape(128, 1),
                                 "c_iotacol").ap()
    c_eps = nc.inline_tensor(np.array([[EPS]], np.float32), "c_eps").ap()

    guW = nc.dram_tensor("guW_loc", [4, JT * 128 * 128], BF16, kind="Internal")
    guWall = nc.dram_tensor("guW_all", [32, JT * 128 * 128], BF16, kind="Internal", addr_space="Shared")
    dnW = nc.dram_tensor("dnW_loc", [11, 8 * 128 * 512], BF16, kind="Internal")
    dnWall = nc.dram_tensor("dnW_all", [88, 8 * 128 * 512], BF16, kind="Internal", addr_space="Shared")
    zg_d = nc.dram_tensor("zg_d", [32, 22016], BF16, kind="Internal")
    zd_d = nc.dram_tensor("zd_d", [96, 4096], BF16, kind="Internal")
    bar_i = nc.dram_tensor("bar_i", [1, 64], F32, kind="Internal")
    bar_o = nc.dram_tensor("bar_o", [1, 64], F32, kind="Internal")

    dbg_outs = {}
    for nm, shape, dt in dbg:
        dbg_outs[nm] = nc.dram_tensor(nm, list(shape), dt, kind="ExternalOutput").ap()

    from contextlib import ExitStack
    with ExitStack() as ctx:
        ARENA_B = 220928
        arena = ctx.enter_context(nc.sbuf_tensor("arena", [128, ARENA_B], mybir.dt.uint8))
        acast = {F32: arena.bitcast(F32), BF16: arena.bitcast(BF16), U32: arena.bitcast(U32)}

        _off = [0]

        def alloc(nbytes, align=512):
            o = (_off[0] + align - 1) // align * align
            _off[0] = o + nbytes
            return o

        def vap(off, dt, rows, dims):
            es = mybir.dt.size(dt)
            assert off % es == 0
            return bass.AP(acast[dt], off // es,
                           [[ARENA_B // es, rows]] + [list(d) for d in dims])

        def view(off, rows, cols, dt):
            return vap(off, dt, rows, [(1, cols)])

        o_X = alloc(65536)
        o_BIG = alloc(98304)
        o_ZONE = alloc(24576)
        o_pool = alloc(6144)
        o_rowz = alloc(6144)
        o_rowbf = alloc(4096)
        o_mrow = o_rowbf + 3072
        o_outb = alloc(4096)
        o_tmpf = alloc(2048)
        o_xg = alloc(1024)
        o_hg = alloc(1024)
        o_zg = alloc(1024)
        o_zd = alloc(2048)
        o_G = alloc(192)
        o_GT = alloc(256)
        o_ident = alloc(256, 64)
        o_mask = alloc(512, 64)
        o_iota32 = alloc(128, 64)
        o_iota86 = alloc(352, 64)
        o_onesrow = alloc(256, 64)
        o_onescol = alloc(256, 64)
        o_shiftv = alloc(512, 64)
        o_bT = {p: alloc(128, 64) for p in "qkvo"}
        o_gidxg = alloc(128, 64)
        o_gidrep = alloc(512, 64)
        o_gidxd = alloc(352, 64)
        o_iotacol = alloc(64, 64)
        o_eps = alloc(64, 64)
        assert _off[0] <= ARENA_B, _off[0]

        def Xt(ct, ch):
            return view(o_X + (ct * T + ch * CH) * 2, 128, CH, BF16)

        def Xcol(ct, c0, w):
            return view(o_X + (ct * T + c0) * 2, 128, w, BF16)

        s_ident = view(o_ident, 128, 128, BF16)
        s_mask = view(o_mask, 128, 128, F32)
        s_iota32 = view(o_iota32, 128, 32, F32)
        s_iota86 = view(o_iota86, 128, 86, F32)
        s_onesrow = view(o_onesrow, 1, 128, BF16)
        s_onescol = view(o_onescol, 128, 1, BF16)
        s_shift = view(o_shiftv, 128, 1, U32)
        s_bT = {p: view(o_bT[p], 128, 32, F32) for p in "qkvo"}
        s_gidxg = view(o_gidxg, 128, 32, F32)
        s_gidrep = view(o_gidrep, 96, 128, F32)
        s_gidxd = view(o_gidxd, 128, 86, F32)
        s_iotacol = view(o_iotacol, 128, 1, F32)
        s_tmpf = view(o_tmpf, 128, 512, F32)
        s_xg = view(o_xg, 32, 512, BF16)
        s_hg = view(o_hg, 86, 512, BF16)
        s_zg = [view(o_zg + i * 512, 32, 256, BF16) for i in range(2)]
        s_zd = [view(o_zd + i * 1024, 96, 512, BF16) for i in range(2)]
        s_G32 = view(o_G, 128, 32, BF16)
        s_G86 = view(o_G, 128, 86, BF16)
        s_GT = view(o_GT, 86, 128, BF16)
        s_pool = [view(o_pool + i * 2048, 128, 512, F32) for i in range(3)]
        s_row = [view(o_rowz + i * 2048, 1, 512, F32) for i in range(3)]
        s_rowbf = [view(o_rowbf + i * 1024, 1, 512, BF16) for i in range(3)]
        s_mrow = view(o_mrow, 1, 512, BF16)
        s_outb = [view(o_outb + i * 2048, 128, 512, F32) for i in range(2)]

        z = o_ZONE
        s_wf32 = [view(z + i * 2048, 128, 512, F32) for i in range(3)]
        s_wbf = [view(z + 6144 + i * 1024, 128, 512, BF16) for i in range(4)]
        o_stage = z + 10240
        s_sS = [view(z + 18432 + i * 512, 128, 128, F32) for i in range(2)]
        s_eS = [view(z + 19456 + i * 256, 128, 128, BF16) for i in range(2)]
        s_pnS = [view(z + 19968 + i * 256, 128, 128, BF16) for i in range(2)]
        s_pTS = [view(z + 20480 + i * 256, 128, 128, BF16) for i in range(2)]
        s_vbS = [view(z + 20992 + i * 256, 128, 128, BF16) for i in range(2)]
        s_rS = view(z + 21504, 128, 4, F32)
        s_guslab = [view(z + i * 8192, 128, 4096, BF16) for i in range(2)]
        s_dnslab = [view(z + 16384 + i * 1024, 128, 512, BF16) for i in range(4)]
        s_gtmp = [view(z + 20480 + i * 1024, 128, 512, BF16) for i in range(2)]
        s_sqt = [view(z + 22528 + i * 1024, 128, 512, BF16) for i in range(2)]

        wb = o_BIG + 65536
        w_qw = view(wb, 128, 2048, U32)
        w_nib = view(wb + 8192, 128, 2048, U32)
        w_sc = view(wb + 16384, 96, 2048, F32)
        w_bc = view(wb + 24576, 128, 2048, BF16)
        w_w1 = view(wb + 28672, 128, 2048, BF16)
        w_znib = view(wb, 96, 2048, U32)
        w_zf = view(wb + 8192, 96, 2048, F32)
        w_zb = view(wb + 24576, 96, 2048, BF16)
        w_scb = view(o_stage, 96, 2048, BF16)
        w_zq = view(o_stage + 4096, 96, 256, U32)
        QTB = o_BIG // 2
        KTB = o_BIG // 2 + 16384
        VTB = o_BIG // 2 + 32768

        def Hbt(ft):
            return view(o_BIG + ft * 1024, 128, 512, BF16)

        bank = [ctx.enter_context(nc.psum_tensor(f"bank{i}", [128, 512], F32))
                for i in range(8)]

        def pb(i, rows=128, cols=512, col0=0):
            return bank[i][0:rows, col0:col0 + cols]

        def pbf(i, rows=128, cols=1024, col0=0):
            return bank[i].bitcast(BF16)[0:rows, col0:col0 + cols]

        for s in ["dS", "dG", "dK", "dD", "tP", "vP", "aP", "gP", "c1", "c2", "c3",
                  "w0", "w1", "w2", "dq", "ds1", "ds2", "dgi", "z1s", "z2s", "z3s",
                  "ga", "gb", "za", "zb", "na", "nb", "nc2", "nd", "ea", "eb",
                  "oa", "ob"]:
            P.handles[s] = ctx.enter_context(nc.semaphore(s))

        def dma(dst, src, q="sync", waits=(), sem=None):
            if sem is None:
                sem = "dS" if q == "sync" else "dG"
            return P.op(q, lambda eng, d=dst, s=src: eng.dma_start(d, s),
                        waits=waits, inc=sem, inc_by=16, barrier=True)

        def mark(engine, fn, waits=()):
            sem = {"tensor": "tP", "vector": "vP", "scalar": "aP", "gpsimd": "gP"}[engine]
            return P.op(engine, fn, waits=waits, inc=sem)

        def quiet(engine, fn, waits=()):
            P.op(engine, fn, waits=waits)

        dbg_names = []

        def dump(name, src_ap, rows, cols, dt, waits=()):
            if name not in DUMP:
                return
            tt = nc.dram_tensor("dbg_" + name, [rows, cols], dt, kind="ExternalOutput")
            dbg_names.append(name)
            d = dma(bass.AP(tt, 0, [[cols, rows], [1, cols]]), src_ap,
                    waits=list(waits) + [("dD", P.v("dD"))], sem="dD")
            for e in ("tensor", "vector", "scalar", "gpsimd"):
                P.op(e, lambda eng: None, waits=[("dD", d)], barrier=True)
        build_program.dbg_names = dbg_names

        for dst, src in [(s_ident, c_ident), (s_mask, c_mask), (s_iota32, c_iota32),
                         (s_iota86, c_iota86),
                         (s_onescol, c_onescol), (s_shift, c_shift),
                         (s_gidxg, gidx_gu), (s_iotacol, c_iotacol),
                         (s_gidxd, gidx_dn)]:
            dma(dst, src, sem="dK")
        dma(s_onesrow, c_onesrow, sem="dK")
        s_eps = view(o_eps, 1, 1, F32)
        nc.const_aps.register(F32, EPS, s_eps)
        dma(s_eps, c_eps, sem="dK")
        for p in "qkvo":
            dma(s_bT[p], bTd[p], sem="dK")
        d_consts = ("dK", P.v("dK"))

        xsrc = bass.AP(xT.tensor, 0, [[T, 128], [128 * T, 32], [1, T]])
        xdst = vap(o_X, BF16, 128, [(T, 32), (1, T)])
        dma(xdst, xsrc, q="gpsimd")
        d_x = P.v("dG")

        state = {
            "wf_free": [0, 0, 0],
            "wf_sem": ["vP", "vP", "vP"],
            "wbf_free": [0, 0, 0, 0],
            "bank_free": [0] * 8,
            "bank_sem": ["vP"] * 8,
            "nconv": 0,
            "cur_pset": None,
            "z3s_total": 0,
        }

        def wbank():
            return 4 if state["cur_pset"] == 0 else 0

        def writer_gen():
            v_zq_free = 0
            v_zsc_free = 0
            d_zst = 0
            for rows, totw, qz_t, qz_w, sc_t, sc_w, zdst_t, zdst_w in [
                    (32, 2752, qz_gu.tensor, 2752, sc_gu.tensor, 22016, zg_d, 22016),
                    (96, 512, qz_dn.tensor, 512, sc_dn.tensor, 4096, zd_d, 4096)]:
                nchunk = (totw + 255) // 256
                for k in range(nchunk):
                    w0 = k * 256
                    nw = min(256, totw - w0)
                    ncol = nw * 8
                    dzq = dma(w_zq[0:rows, 0:nw],
                              bass.AP(qz_t, w0, [[qz_w, rows], [1, nw]]),
                              waits=[("vP", v_zq_free), d_consts], sem="z1s", q="scalar")
                    dsc = dma(w_sc[0:rows, 0:ncol],
                              bass.AP(sc_t, w0 * 8, [[sc_w, rows], [1, ncol]]),
                              waits=[("vP", v_zsc_free)], sem="z2s", q="scalar")
                    for s in range(8):
                        ap_o = bass.AP(acast[U32], wb // 4 + s,
                                       [[ARENA_B // 4, rows], [8, nw]])
                        quiet("vector", lambda eng, d=ap_o, sh=4 * s, r=rows, n=nw: eng.tensor_scalar(
                            d, w_zq[0:r, 0:n], sh, 15,
                            OP.logical_shift_right, OP.bitwise_and),
                            waits=[("z1s", dzq)] if s == 0 else [])
                    v_zq_free = P.v("vP")
                    quiet("vector", lambda eng, r=rows, n=ncol: eng.tensor_scalar(
                        w_zf[0:r, 0:n], w_znib[0:r, 0:n], 1.0, -1.0, OP.add, OP.mult))
                    v_zb = mark("vector", lambda eng, r=rows, n=ncol: eng.tensor_tensor(
                        w_zb[0:r, 0:n], w_zf[0:r, 0:n], w_sc[0:r, 0:n], OP.mult),
                        waits=[("z2s", dsc), ("z3s", d_zst)])
                    v_zsc_free = v_zb
                    d_zst = dma(bass.AP(zdst_t, w0 * 8, [[zdst_w, rows], [1, ncol]]),
                                w_zb[0:rows, 0:ncol], waits=[("vP", v_zb)],
                                sem="z3s", q="scalar")
                    yield

            slices = []
            for it in range(4):
                j0 = 0
                for jw in [2048] * 10 + [1536]:
                    slices.append(("gu", it, j0, jw))
                    j0 += jw
            for kk in range(11):
                for sl in range(2):
                    slices.append(("dn", kk, sl * 2048, 2048))

            def emit_loads(idx, waits_qw, waits_sc):
                kind, blk, j0, jw = slices[idx]
                if kind == "gu":
                    qw_t, qw_rs, qw_off = qw_gu.tensor, 22016, blk * 16 * 22016 + j0
                    sc_t, sc_rs, sc_off, drows = sc_gu.tensor, 22016, j0, 32
                else:
                    qw_t, qw_rs, qw_off = qw_dn.tensor, 4096, blk * 16 * 4096 + j0
                    sc_t, sc_rs, sc_off, drows = sc_dn.tensor, 4096, j0, 96
                dq = 0
                for rpl in range(8):
                    dst = bass.AP(acast[U32], wb // 4 + rpl * (ARENA_B // 4),
                                  [[(ARENA_B // 4) * 8, 16], [1, jw]])
                    dq = dma(dst, bass.AP(qw_t, qw_off, [[qw_rs, 16], [1, jw]]),
                             waits=(list(waits_qw) if rpl == 0 else []),
                             sem="dq", q="scalar")
                dsc = dma(w_sc[0:drows, 0:jw],
                          bass.AP(sc_t, sc_off, [[sc_rs, drows], [1, jw]]),
                          waits=list(waits_sc), sem="ds1", q="scalar")
                return dq, dsc

            def emit_gid(idx, dma_waits, vgt_waits):
                kind, blk, _, _ = slices[idx]
                rows = 32 if kind == "gu" else 86
                src_t = gidx_guT.tensor if kind == "gu" else gidx_dnT.tensor
                dg = dma(s_gidrep[0:rows, 0:128],
                         bass.AP(src_t, blk * 128, [[0, rows], [1, 128]]),
                         waits=list(dma_waits), sem="dgi", q="scalar")
                return mark("vector", lambda eng, r=rows: eng.tensor_scalar(
                    s_GT[0:r, 0:128], s_gidrep[0:r, 0:128], s_iotacol[0:r, 0:1],
                    None, OP.is_equal),
                    waits=[("dgi", dg)] + list(vgt_waits))

            vGT = emit_gid(0, [d_consts], [])
            dq, dsc = emit_loads(0, [("vP", P.v("vP"))], [("vP", P.v("vP"))])
            t_bc = 0
            a_bc = 0
            d_st = 0
            first_bc = True
            for k in range(len(slices)):
                kind, blk, j0, jw = slices[k]
                rows = 32 if kind == "gu" else 86
                a_scb = mark("scalar", lambda eng, r=rows, n=jw: eng.activation(
                    w_scb[0:r, 0:n], w_sc[0:r, 0:n], AF.Copy),
                    waits=[("ds1", dsc), ("tP", t_bc)])
                for q4 in (() if WR_NO_BC else range(jw // 512)):
                    bk = wbank()
                    t_bc = mark("tensor", lambda eng, b=bk, q=q4, r=rows: eng.matmul(
                        pb(b), s_GT[0:r, 0:128], w_scb[0:r, q * 512:(q + 1) * 512],
                        start=True, stop=True),
                        waits=[("aP", a_scb), ("vP", vGT),
                               (state["bank_sem"][bk], state["bank_free"][bk])])
                    evw = [("tP", t_bc)]
                    if first_bc:
                        evw.append(("z3s", P.v("z3s")))
                        first_bc = False
                    a_bc = mark("scalar", lambda eng, b=bk, q=q4: eng.activation(
                        w_bc[:, q * 512:(q + 1) * 512], pb(b), AF.Copy), waits=evw)
                    state["bank_free"][bk] = a_bc
                    state["bank_sem"][bk] = "aP"
                if not WR_NO_EXT:
                    v_un = mark("vector", lambda eng, n=jw: eng.tensor_scalar(
                        w_nib[:, 0:n], w_qw[:, 0:n], s_shift, 15,
                        OP.logical_shift_right, OP.bitwise_and),
                        waits=[("dq", dq)])
                    v_w1 = mark("vector", lambda eng, n=jw: eng.tensor_tensor(
                        w_w1[:, 0:n], w_nib[:, 0:n], w_bc[:, 0:n], OP.mult),
                        waits=[("ds2", d_st), ("aP", a_bc)])
                else:
                    v_un = P.v("vP")
                    v_w1 = P.v("vP")
                if kind == "gu":
                    dstap = bass.AP(guW, blk * 128 * 22016 + j0,
                                    [[22016, 128], [1, jw]])
                    srcap = view(wb + 28672, 128, jw, BF16)
                else:
                    dstap = bass.AP(dnW, blk * (8 * 65536) + (j0 // 512) * 65536,
                                    [[512, 128], [65536, jw // 512], [1, 512]])
                    srcap = vap(wb + 28672, BF16, 128, [(512, jw // 512), (1, 512)])
                if not WR_NO_STORE:
                    d_st = dma(dstap, srcap, waits=[("vP", v_w1)], sem="ds2", q="scalar")
                if k + 1 < len(slices):
                    nkind, nblk = slices[k + 1][0], slices[k + 1][1]
                    if (nkind, nblk) != (kind, blk) and not WR_NO_GID:
                        vGT = emit_gid(k + 1, [("vP", vGT)], [("tP", t_bc)])
                    dq, dsc = emit_loads(k + 1, [("vP", v_un)], [("aP", a_scb)])
                yield

        w_gen = writer_gen()
        wr_cnt = [0]

        def pump(n=4):
            for _ in range(n):
                if wr_cnt[0] >= WR_LIMIT:
                    break
                if next(w_gen, "END") == "END":
                    break
                wr_cnt[0] += 1

        def writer_drain():
            while wr_cnt[0] < WR_LIMIT:
                if next(w_gen, "END") == "END":
                    break
                wr_cnt[0] += 1
            state["z3s_total"] = P.v("z3s")
            wrv = [(nm, P.v(nm)) for nm in
                   ("dq", "ds1", "ds2", "z2s", "z3s")] + [("aP", P.v("aP"))]
            P.op("gpsimd", lambda eng: eng.collective_compute(
                "AllGather", OP.bypass, replica_groups=[list(range(8))],
                ins=[guW.ap().opt()], outs=[guWall.ap().opt()]),
                waits=[("ds2", P.v("ds2"))], inc="c1")
            state["ds2_total"] = P.v("ds2")
            P.op("gpsimd", lambda eng: eng.collective_compute(
                "AllReduce", OP.add, replica_groups=[list(range(8))],
                ins=[bar_i.ap().opt()], outs=[bar_o.ap().opt()]), inc="c3")
            dump("guw", bass.AP(guW.bitcast(F32), 0, [[JT * 8192, 4], [1, JT * 8192]]),
                 4, JT * 8192, F32, waits=list(wrv))
            dump("dnw", bass.AP(dnW.bitcast(F32), 0, [[8 * 32768, 11], [1, 8 * 32768]]),
                 11, 8 * 32768, F32, waits=list(wrv))
            dump("zg", bass.AP(zg_d.bitcast(F32), 0, [[11008, 32], [1, 11008]]),
                 32, 11008, F32, waits=list(wrv))
            dump("zd", bass.AP(zd_d.bitcast(F32), 0, [[2048, 86], [1, 2048]]),
                 86, 2048, F32, waits=list(wrv))
            return wrv

        def ln_phase(ch, which, extra_first=()):
            first_w = list(extra_first) + [d_consts, ("dG", d_x)]
            t_s1 = t_s2 = 0
            a_sq = 0
            for ct in range(32):
                a_sq = mark("scalar", lambda eng, c=ct: eng.activation(
                    s_sqt[c % 2], Xt(c, ch), AF.Square),
                    waits=(first_w if ct == 0 else [("tP", t_s2)]))
                w0 = []
                if ct == 0:
                    w0 = first_w + [(state["bank_sem"][i], state["bank_free"][i])
                                    for i in (0, 1, 2)]
                t_s1 = mark("tensor", lambda eng, c=ct: eng.matmul(
                    pb(0, 1, 512), s_onescol, Xt(c, ch),
                    start=(c == 0), stop=(c == 31)), waits=w0)
                t_s2 = mark("tensor", lambda eng, c=ct: eng.matmul(
                    pb(1, 1, 512), s_onescol, s_sqt[c % 2],
                    start=(c == 0), stop=(c == 31)), waits=[("aP", a_sq)])
            quiet("vector", lambda eng: eng.tensor_scalar(
                s_row[0], pb(0, 1, 512), 1.0 / HID, None, OP.mult),
                waits=[("tP", t_s1)])
            quiet("vector", lambda eng: eng.tensor_copy(s_mrow, s_row[0]))
            quiet("vector", lambda eng: eng.tensor_scalar(
                s_row[1], pb(1, 1, 512), 1.0 / HID, None, OP.mult),
                waits=[("tP", t_s2)])
            quiet("vector", lambda eng: eng.tensor_tensor(
                s_row[2], s_row[0], s_row[0], OP.mult))
            v_var = mark("vector", lambda eng: eng.tensor_tensor(
                s_row[1], s_row[1], s_row[2], OP.subtract))
            a_sqr = mark("scalar", lambda eng: eng.activation(
                s_row[2], s_row[1], AF.Sqrt, bias=EPS), waits=[("vP", v_var)])
            quiet("vector", lambda eng: eng.tensor_copy(s_rowbf[0], s_row[2]),
                  waits=[("aP", a_sqr)])
            quiet("vector", lambda eng: eng.reciprocal(s_row[1], s_row[2]))
            quiet("vector", lambda eng: eng.tensor_copy(s_rowbf[1], s_row[1]))
            quiet("vector", lambda eng: eng.scalar_tensor_tensor(
                s_row[2], s_row[0], -1.0, s_row[1], OP.mult, OP.mult))
            v_rows = mark("vector", lambda eng: eng.tensor_copy(s_rowbf[2], s_row[2]))
            v_bc = 0
            for slot, rb in ((0, 1), (1, 2), (2, 0)):
                t_b = mark("tensor", lambda eng, r=rb: eng.matmul(
                    pb(2), s_onesrow, s_rowbf[r], start=True, stop=True),
                    waits=[("vP", max(v_rows, v_bc))])
                v_bc = mark("vector", lambda eng, sl=slot: eng.tensor_copy(
                    s_pool[sl], pb(2)), waits=[("tP", t_b)])
            for i in (0, 1, 2):
                state["bank_free"][i] = v_bc
                state["bank_sem"][i] = "vP"
            v_tr = 0
            for ct in range(32):
                quiet("vector", lambda eng, c=ct: eng.tensor_tensor(
                    s_tmpf, Xt(c, ch), s_pool[0], OP.mult),
                    waits=[("vP", v_bc)] if ct == 0 else [])
                v_tr = mark("vector", lambda eng, c=ct: eng.tensor_tensor(
                    Xt(c, ch), s_tmpf, s_pool[1], OP.add))
            return v_tr

        def proj_phase(ch, p, rhs_base, dst_base, extra_mm_wait=(), extra_evac_wait=(),
                       fold_mrow=False, recon=False, pump_fn=None):
            v_ev = 0
            for og in range(8):
                pset = (og % 2) * 4
                state["cur_pset"] = pset
                if pump_fn is not None:
                    pump_fn(4)
                tmm = 0
                for ct in range(32):
                    n = state["nconv"]
                    i3 = n % 3
                    j4 = n % 4
                    state["nconv"] = n + 1
                    if PROJ_SERIAL == "tP":
                        ser = [("tP", P.v("tP"))]
                    elif PROJ_SERIAL == "vP":
                        ser = [("vP", P.v("vP"))]
                    elif PROJ_SERIAL:
                        ser = [("tP", P.v("tP")), ("vP", P.v("vP"))]
                    else:
                        ser = []
                    dW = dma(s_wf32[i3],
                             bass.AP(WTd[p].tensor, ct * 128 * HID + og * 512,
                                     [[HID, 128], [1, 512]]),
                             waits=[(state["wf_sem"][i3], state["wf_free"][i3])] + ser,
                             sem=f"w{i3}")
                    if n % 2 == 0:
                        cm = mark("scalar", lambda eng, a=j4, b=i3: eng.activation(
                            s_wbf[a], s_wf32[b], AF.Copy),
                            waits=[(f"w{i3}", dW), ("tP", state["wbf_free"][j4])])
                        csem = "aP"
                    else:
                        cm = mark("vector", lambda eng, a=j4, b=i3: eng.tensor_copy(
                            s_wbf[a], s_wf32[b]),
                            waits=[(f"w{i3}", dW), ("tP", state["wbf_free"][j4])])
                        csem = "vP"
                    state["wf_free"][i3] = cm
                    state["wf_sem"][i3] = csem
                    if rhs_base is None:
                        rhs = Xt(ct, ch)
                    else:
                        rhs = bass.AP(acast[BF16], rhs_base + ct * 128,
                                      [[ARENA_B // 2, 128], [4096, 4], [1, 128]])
                    for ot in range(4):
                        bk = pset + ot
                        w = [(csem, cm)] if ot == 0 else []
                        if ct == 0:
                            w.append((state["bank_sem"][bk], state["bank_free"][bk]))
                            w += list(extra_mm_wait)
                        tmm = mark("tensor", lambda eng, b=bk, j=j4, o=ot, r=rhs, c=ct: eng.matmul(
                            pb(b), s_wbf[j][:, o * 128:(o + 1) * 128], r,
                            start=(c == 0), stop=(c == 31 and not fold_mrow)), waits=w)
                    state["wbf_free"][j4] = tmm
                if fold_mrow:
                    for ot in range(4):
                        tmm = mark("tensor", lambda eng, b=pset + ot: eng.matmul(
                            pb(b), s_onesrow, s_mrow, start=False, stop=True))
                for ot in range(4):
                    bk = pset + ot
                    o = og * 4 + ot
                    ew = list(extra_evac_wait) if (og == 0 and ot == 0) else []
                    if not recon:
                        dstap = bass.AP(acast[BF16], dst_base + o,
                                        [[ARENA_B // 2, 128], [128, 128], [32, 4]])
                        srcap = bass.AP(bank[bk], 0, [[512, 128], [4, 128], [1, 4]])
                        v_ev = mark("vector", lambda eng, sp=srcap, oo=o, d=dstap: eng.tensor_scalar(
                            d, sp, s_bT[p][:, oo:oo + 1], None, OP.add),
                            waits=[("tP", tmm)] + ew)
                    else:
                        quiet("vector", lambda eng, oo=o: eng.tensor_tensor(
                            s_tmpf, Xt(oo, ch), s_pool[2], OP.mult),
                            waits=[("tP", tmm)] + ew)
                        v_ev = mark("vector", lambda eng, b=bk, oo=o: eng.scalar_tensor_tensor(
                            Xt(oo, ch), pb(b), s_bT[p][:, oo:oo + 1], s_tmpf,
                            OP.add, OP.add))
                    state["bank_free"][bk] = v_ev
                    state["bank_sem"][bk] = "vP"
                if og == 0 and p == "q" and ch == 0:
                    wv = [("vP", P.v("vP"))]
                    dump("qog0", view(o_BIG, 128, 8192, F32), 128, 8192,
                         F32, waits=wv)
                    dump("zwf", view(o_ZONE, 128, 1536, F32), 128, 1536,
                         F32, waits=wv)
                    dump("zwb", view(o_ZONE + 6144, 128, 1024, F32), 128, 1024,
                         F32, waits=wv)
            return v_ev

        def attn_core(ch, v_qkv_done, extra_first=()):
            t_sc = {}
            t_pt = {}
            t_vb = {}
            t_av = {}
            m_sstt = {}
            m_exp = {}
            m_v0 = {}
            m_v1 = {}
            m_v2 = {}
            m_v3 = {}
            v_deint = 0
            for G in range(4):
                g0 = G * 32
                for b in range(33):
                    gb = g0 + b
                    if b < 32:
                        r = gb % 2
                        qb = bass.AP(acast[BF16], QTB + gb * 128,
                                     [[ARENA_B // 2, 128], [1, 128]])
                        kb = bass.AP(acast[BF16], KTB + gb * 128,
                                     [[ARENA_B // 2, 128], [1, 128]])
                        vbp = bass.AP(acast[BF16], VTB + gb * 128,
                                      [[ARENA_B // 2, 128], [1, 128]])
                        w = [("vP", m_sstt.get(gb - 2, 0))]
                        if gb == 0:
                            w += [("vP", v_qkv_done)] + list(extra_first)
                        t_sc[gb] = mark("tensor", lambda eng, q=qb, k=kb, r2=r: eng.matmul(
                            pb(0 + r2, 128, 128), q, k, start=True, stop=True), waits=w)
                        t_vb[gb] = mark("tensor", lambda eng, v=vbp, r2=r: eng.transpose(
                            pbf(4 + r2, 128, 128), v, s_ident),
                            waits=[("vP", m_v2.get(gb - 2, 0))])
                    if b >= 1:
                        gbb = g0 + b - 1
                        r2 = gbb % 2
                        t_pt[gbb] = mark("tensor", lambda eng, r3=r2: eng.transpose(
                            pbf(2 + r3, 128, 128), s_eS[r3], s_ident),
                            waits=[("vP", m_v0[gbb]), ("vP", m_v1.get(gbb - 2, 0))])
                        m_v1[gbb] = mark("vector", lambda eng, r3=r2: eng.tensor_copy(
                            s_pTS[r3], pbf(2 + r3, 128, 128)),
                            waits=[("tP", t_pt[gbb]), ("tP", t_av.get(gbb - 2, 0))])
                        m_v2[gbb] = mark("vector", lambda eng, r3=r2: eng.tensor_copy(
                            s_vbS[r3], pbf(4 + r3, 128, 128)),
                            waits=[("tP", t_vb[gbb])])
                        t_av[gbb] = mark("tensor", lambda eng, r3=r2: eng.matmul(
                            pb(6 + r3, 128, 128), s_vbS[r3], s_pTS[r3],
                            start=True, stop=True),
                            waits=[("vP", m_v2[gbb]),
                                   ("vP", m_v3.get(gbb - 2, 0))])
                    if b < 32:
                        r2 = gb % 2
                        m_sstt[gb] = mark("vector", lambda eng, r3=r2: eng.scalar_tensor_tensor(
                            s_sS[r3], pb(0 + r3, 128, 128), ISD, s_mask, OP.mult, OP.add),
                            waits=[("tP", t_sc[gb]), ("aP", m_exp.get(gb - 2, 0))])
                        m_exp[gb] = mark("scalar", lambda eng, r3=r2: eng.activation(
                            s_eS[r3], s_sS[r3], AF.Exp),
                            waits=[("vP", m_sstt[gb]),
                                   ("tP", t_pt.get(gb - 2, 0))])
                        quiet("vector", lambda eng, r3=r2: eng.scalar_tensor_tensor(
                            s_pnS[r3], s_sS[r3], THRESH, s_eS[r3], OP.is_gt, OP.mult),
                            waits=[("aP", m_exp[gb])])
                        quiet("vector", lambda eng, r3=r2: eng.reduce_sum(
                            s_rS[:, r3:r3 + 1], s_pnS[r3], axis=AX.X))
                        quiet("vector", lambda eng, r3=r2: eng.reciprocal(
                            s_rS[:, 2 + r3:3 + r3], s_rS[:, r3:r3 + 1]))
                        m_v0[gb] = mark("vector", lambda eng, r3=r2: eng.tensor_scalar(
                            s_eS[r3], s_pnS[r3], s_rS[:, 2 + r3:3 + r3], None, OP.mult))
                    if b >= 1:
                        gbb = g0 + b - 1
                        r2 = gbb % 2
                        m_v3[gbb] = mark("vector", lambda eng, r3=r2, b2=b - 1: eng.tensor_copy(
                            vap(o_stage + b2 * 256, BF16, 128, [(1, 128)]),
                            pb(6 + r3, 128, 128)),
                            waits=[("tP", t_av[gbb])])
                for h in range(32):
                    src = bass.AP(acast[BF16], o_stage // 2 + h,
                                  [[ARENA_B // 2, 128], [128, 32], [32, 4]])
                    dst = bass.AP(acast[BF16], KTB + G * 4096 + h * 128,
                                  [[ARENA_B // 2, 128], [4, 32], [1, 4]])
                    hw = [("tP", t_sc[g0 + 31]), ("vP", m_v3[g0 + 31])] if h == 0 else []
                    if h == 31:
                        v_deint = mark("vector",
                                       lambda eng, s2=src, d2=dst: eng.tensor_copy(d2, s2),
                                       waits=hw)
                    else:
                        quiet("vector", lambda eng, s2=src, d2=dst: eng.tensor_copy(d2, s2),
                              waits=hw)
            for i in range(8):
                state["bank_free"][i] = P.v("vP")
                state["bank_sem"][i] = "vP"
            return v_deint

        def ffn_phase(ch, v_ln2_done, first_gu_waits):
            if ch == 0:
                P.op("gpsimd", lambda eng: eng.collective_compute(
                    "AllGather", OP.bypass, replica_groups=[list(range(8))],
                    ins=[dnW.ap().opt()], outs=[dnWall.ap().opt()]),
                    waits=[("ds2", state["ds2_total"])], inc="c2")
            t_x = 0
            for ct in range(32):
                vg = mark("vector", lambda eng, c=ct: eng.tensor_scalar(
                    s_G32, s_iota32, s_gidxg[:, c:c + 1], None, OP.is_equal),
                    waits=[("tP", t_x), ("vP", v_ln2_done if ct == 0 else 0)])
                wx = [("vP", vg)]
                if ct == 0:
                    wx.append((state["bank_sem"][4], state["bank_free"][4]))
                t_x = mark("tensor", lambda eng, c=ct: eng.matmul(
                    pb(4, 32, 512), s_G32, Xt(c, ch), start=(c == 0), stop=(c == 31)),
                    waits=wx)
            v_xg = mark("vector", lambda eng: eng.tensor_copy(s_xg, pb(4, 32, 512)),
                        waits=[("tP", t_x)])
            slab_free = [0, 0]
            gate_free = [0, 0]
            up_free = [0, 0]
            zg_free = [0, 0]
            a_silu = {}
            v_h = {}
            d_zg = {}
            for idx in range(2 * FT):
                ft = idx // 2
                is_up = idx % 2
                jt = ft + 86 * is_up
                sl = idx % 2
                w = [("tP", slab_free[sl])]
                if idx == 0:
                    w += list(first_gu_waits) + [("c1", 1), ("c3", 1)]
                src = bass.AP(guWall, jt * 128,
                              [[22016, 128], [JT * 16384, 32], [1, 128]])
                gsem = "ga" if sl == 0 else "gb"
                dsl = dma(vap(z + sl * 8192, BF16, 128, [(128, 32), (1, 128)]), src,
                          waits=w, sem=gsem)
                if not is_up:
                    zsem = "za" if ft % 2 == 0 else "zb"
                    zw = [("tP", zg_free[ft % 2])]
                    if ft < 2:
                        zw.append(("z3s", state["z3s_total"]))
                    z1 = dma(s_zg[ft % 2][0:32, 0:128],
                             bass.AP(zg_d, ft * 128, [[22016, 32], [1, 128]]),
                             waits=zw, sem=zsem, q="sync")
                    z2 = dma(s_zg[ft % 2][0:32, 128:256],
                             bass.AP(zg_d, (86 + ft) * 128, [[22016, 32], [1, 128]]),
                             sem=zsem, q="sync")
                    d_zg[ft] = (zsem, z2)
                bk = (2 if is_up else 0) + ft % 2
                wmm0 = [(gsem, dsl)]
                if not is_up:
                    wmm0.append(("aP", gate_free[ft % 2]))
                else:
                    wmm0.append(("vP", up_free[ft % 2]))
                tmm = 0
                for ct in range(32):
                    tmm = mark("tensor", lambda eng, b=bk, c=ct, s2=sl: eng.matmul(
                        pb(b), s_guslab[s2][:, c * 128:(c + 1) * 128], Xt(c, ch),
                        start=(c == 0), stop=False),
                        waits=(wmm0 if ct == 0 else []))
                tz = mark("tensor", lambda eng, b=bk, f=ft, u=is_up: eng.matmul(
                    pb(b), s_zg[f % 2][0:32, u * 128:(u + 1) * 128], s_xg,
                    start=False, stop=True),
                    waits=[d_zg[ft], ("vP", v_xg)])
                slab_free[sl] = tz
                if not is_up:
                    a_silu[ft] = mark("scalar", lambda eng, b=bk, f=ft: eng.activation(
                        s_gtmp[f % 2], pb(b), AF.Silu),
                        waits=[("tP", tz), ("vP", v_h.get(ft - 2, 0))])
                    gate_free[ft % 2] = a_silu[ft]
                else:
                    v_h[ft] = mark("vector", lambda eng, b=bk, f=ft: eng.tensor_tensor(
                        Hbt(f), s_gtmp[f % 2], pb(b), OP.mult),
                        waits=[("tP", tz), ("aP", a_silu[ft])])
                    up_free[ft % 2] = v_h[ft]
                    zg_free[ft % 2] = tz
            state["bank_free"][0] = a_silu[84]; state["bank_sem"][0] = "aP"
            state["bank_free"][1] = a_silu[85]; state["bank_sem"][1] = "aP"
            state["bank_free"][2] = v_h[84]; state["bank_sem"][2] = "vP"
            state["bank_free"][3] = v_h[85]; state["bank_sem"][3] = "vP"
            t_hg = 0
            for ft in range(FT):
                vg = mark("vector", lambda eng, c=ft: eng.tensor_scalar(
                    s_G86, s_iota86, s_gidxd[:, c:c + 1], None, OP.is_equal),
                    waits=[("tP", t_hg)])
                t_hg = mark("tensor", lambda eng, c=ft: eng.matmul(
                    pb(4, 86, 512), s_G86, Hbt(c), start=(c == 0), stop=(c == 85)),
                    waits=[("vP", max(vg, v_h[ft]))] +
                          ([("vP", v_xg)] if ft == 0 else []))
            v_hg = mark("vector", lambda eng: eng.tensor_copy(s_hg, pb(4, 86, 512)),
                        waits=[("tP", t_hg)])
            state["bank_free"][4] = v_hg
            state["bank_sem"][4] = "vP"
            dn_free = [0, 0, 0, 0]
            zd_free = [0, 0]
            v_ev = 0
            for og in range(8):
                pset = (og % 2) * 4
                esem = "ea" if og % 2 == 0 else "eb"
                zdw = [("tP", zd_free[og % 2])]
                if og < 2:
                    zdw.append(("z3s", state["z3s_total"]))
                d_zd = dma(s_zd[og % 2],
                           bass.AP(zd_d, og * 512, [[4096, 96], [1, 512]]),
                           waits=zdw, sem=esem, q="sync")
                for ot in range(4):
                    bk = pset + ot
                    mark("tensor", lambda eng, b=bk: eng.matmul(
                        pb(b), s_onesrow, s_mrow, start=True, stop=False),
                        waits=[(state["bank_sem"][bk], state["bank_free"][bk])] +
                              ([("vP", v_hg)] if og == 0 and ot == 0 else []))
                tmm = 0
                for ft in range(FT):
                    k4 = (og * FT + ft) % 4
                    nsem = ("na", "nb", "nc2", "nd")[k4]
                    dsl = dma(s_dnslab[k4],
                              bass.AP(dnWall, ft * (8 * 65536) + og * 65536,
                                      [[512, 128], [1, 512]]),
                              waits=[("tP", dn_free[k4])] +
                                    ([("c2", 1)] if og == 0 and ft == 0 else []),
                              sem=nsem, q="sync")
                    for ot in range(4):
                        bk = pset + ot
                        tmm = mark("tensor", lambda eng, b=bk, k=k4, o=ot, f=ft: eng.matmul(
                            pb(b), s_dnslab[k][:, o * 128:(o + 1) * 128], Hbt(f),
                            start=False, stop=False),
                            waits=[(nsem, dsl)] if ot == 0 else [])
                    dn_free[k4] = tmm
                tz = 0
                for ot in range(4):
                    bk = pset + ot
                    tz = mark("tensor", lambda eng, b=bk, o=ot, g=og: eng.matmul(
                        pb(b), s_zd[g % 2][0:86, o * 128:(o + 1) * 128], s_hg,
                        start=False, stop=True),
                        waits=[(esem, d_zd), ("vP", v_hg)] if ot == 0 else [])
                zd_free[og % 2] = tz
                for ot in range(4):
                    bk = pset + ot
                    o = og * 4 + ot
                    quiet("vector", lambda eng, oo=o: eng.tensor_tensor(
                        s_tmpf, Xt(oo, ch), s_pool[2], OP.mult), waits=[("tP", tz)])
                    v_ev = mark("vector", lambda eng, b=bk, oo=o: eng.tensor_tensor(
                        Xt(oo, ch), pb(b), s_tmpf, OP.add))
                    state["bank_free"][bk] = v_ev
                    state["bank_sem"][bk] = "vP"
            return v_ev

        def out_phase(ch, v_x2_done):
            st_free = [0, 0]
            for tt in range(4):
                for g4 in range(8):
                    r = g4 % 2
                    t_tr = 0
                    for ci in range(4):
                        ct = g4 * 4 + ci
                        w = []
                        if ci == 0:
                            w = [("vP", v_x2_done),
                                 (state["bank_sem"][r], state["bank_free"][r])]
                        t_tr = mark("tensor", lambda eng, c=ct, i=ci, r2=r,
                                    c0=ch * CH + tt * 128: eng.transpose(
                            pbf(r2, 128, 128, col0=i * 128),
                            Xcol(c, c0, 128), s_ident), waits=w)
                    osem = "oa" if r == 0 else "ob"
                    v_e = mark("vector", lambda eng, r2=r: eng.tensor_copy(
                        s_outb[r2], pbf(r2, 128, 512)),
                        waits=[("tP", t_tr), (osem, st_free[r])])
                    state["bank_free"][r] = v_e
                    state["bank_sem"][r] = "vP"
                    dst = bass.AP(out.tensor, (ch * CH + tt * 128) * HID + g4 * 512,
                                  [[HID, 128], [1, 512]])
                    P.op("sync", lambda eng, d=dst, r2=r: eng.dma_start(d, s_outb[r2]),
                         waits=[("vP", v_e)], inc=osem, inc_by=16)
                    st_free[r] = P.v(osem)

        prev = {"t_dn_last": 0}
        for ch in range(2):
            if ch == 0:
                evac_gate = []
                ln1_first = []
            else:
                evac_gate = [("tP", prev["t_dn_last"])]
                ln1_first = [("tP", prev["t_dn_last"])]
            v_ln1 = ln_phase(ch, 1, extra_first=ln1_first)
            if ch == 0:
                state["cur_pset"] = None
                pump(4)
                dump("ln1", view(o_X, 128, 16384, F32), 128, 16384, F32,
                     waits=[("vP", v_ln1)])
                dump("rows1", view(o_rowz, 1, 1536, F32), 1, 1536, F32,
                     waits=[("vP", v_ln1)])
                dump("pool1", view(o_pool, 128, 1536, F32), 128, 1536, F32,
                     waits=[("vP", v_ln1)])
            v_q = proj_phase(ch, "q", None, QTB, extra_mm_wait=[("vP", v_ln1)],
                             extra_evac_wait=evac_gate,
                             pump_fn=pump if ch == 0 else None)
            if ch == 0:
                dump("qbuf", view(o_BIG, 128, 8192, F32), 128, 8192, F32,
                     waits=[("vP", v_q)])
                dump("qbuf2", view(o_X, 128, 16384, F32), 128, 16384, F32,
                     waits=[("vP", v_q)])
            v_k = proj_phase(ch, "k", None, KTB,
                             pump_fn=pump if ch == 0 else None)
            if ch == 0:
                v_gate = writer_drain()
            else:
                v_gate = []
            v_v = proj_phase(ch, "v", None, VTB, extra_evac_wait=v_gate)
            if ch == 0:
                dump("kbuf", view(o_BIG + 32768, 128, 8192, F32), 128, 8192,
                     F32, waits=[("vP", v_v)])
                dump("vbuf", view(o_BIG + 65536, 128, 8192, F32), 128, 8192,
                     F32, waits=[("vP", v_v)])
            core_extra = [("tP", prev["t_dn_last"])] if ch == 1 else []
            v_deint = attn_core(ch, v_v, extra_first=core_extra)
            if ch == 0:
                dump("abuf", view(o_BIG + 32768, 128, 8192, F32), 128, 8192,
                     F32, waits=[("vP", v_deint)])
            v_wo = proj_phase(ch, "o", KTB, None, fold_mrow=True, recon=True,
                              extra_mm_wait=[("vP", v_deint)])
            if ch == 0:
                dump("x1", view(o_X, 128, 16384, F32), 128, 16384, F32,
                     waits=[("vP", v_wo)])
            v_ln2 = ln_phase(ch, 2, extra_first=[("vP", v_wo)])
            if ch == 0:
                dump("ln2", view(o_X, 128, 16384, F32), 128, 16384, F32,
                     waits=[("vP", v_ln2)])
                dump("rows2", view(o_rowz, 1, 1536, F32), 1, 1536, F32,
                     waits=[("vP", v_ln2)])
            first_gu = [("vP", v_deint), ("tP", P.v("tP"))]
            v_x2 = ffn_phase(ch, v_ln2, first_gu)
            if ch == 0:
                dump("xgbuf", view(o_xg, 32, 256, F32), 32, 256, F32,
                     waits=[("vP", v_x2)])
                dump("hgbuf", view(o_hg, 86, 256, F32), 86, 256, F32,
                     waits=[("vP", v_x2)])
                dump("hbuf", view(o_BIG, 128, 86 * 256, F32), 128, 86 * 256,
                     F32, waits=[("vP", v_x2)])
                dump("x2", view(o_X, 128, 16384, F32), 128, 16384, F32,
                     waits=[("vP", v_x2)])
            prev["t_dn_last"] = P.v("tP")
            out_phase(ch, v_x2)

        P.op("sync", lambda eng: None,
             waits=[("oa", P.v("oa")), ("ob", P.v("ob"))])

        for nm in dbg_outs:
            pass

        with nc.Block() as block:
            @block.sync
            def _(eng):
                for e, f in P.ops:
                    if e == "sync":
                        f(eng)

            @block.tensor
            def _(eng):
                for e, f in P.ops:
                    if e == "tensor":
                        f(eng)

            @block.vector
            def _(eng):
                for e, f in P.ops:
                    if e == "vector":
                        f(eng)

            @block.scalar
            def _(eng):
                for e, f in P.ops:
                    if e == "scalar":
                        f(eng)

            @block.gpsimd
            def _(eng):
                for e, f in P.ops:
                    if e == "gpsimd":
                        f(eng)

    return nc


def kernel(**inputs):
    f32 = np.float32
    x = np.asarray(inputs["L0_attn_x"], f32).reshape(8192, HID)
    xT_full = np.ascontiguousarray(x.T)
    WT = {p: np.ascontiguousarray(np.asarray(inputs[f"L0_attn_W{p}"], f32).T)
          for p in "qkvo"}
    bT = {p: np.ascontiguousarray(np.asarray(inputs[f"L0_attn_b{p}"], f32)
                                  .reshape(32, 128).T) for p in "qkvo"}
    qwg = np.asarray(inputs["L0_ffn_qweight_gu"], np.uint32)
    qzg = np.asarray(inputs["L0_ffn_qzeros_gu"], np.uint32)
    scg = np.asarray(inputs["L0_ffn_scales_gu"], f32)
    gig = np.asarray(inputs["L0_ffn_g_idx_gu"]).astype(f32)
    qwd = np.asarray(inputs["L0_ffn_qweight_down"], np.uint32)
    qzd = np.asarray(inputs["L0_ffn_qzeros_down"], np.uint32)
    scd = np.asarray(inputs["L0_ffn_scales_down"], f32)
    gid = np.asarray(inputs["L0_ffn_g_idx_down"]).astype(f32)

    gidxg = np.ascontiguousarray(gig.reshape(32, 128).T)
    gidxd = np.ascontiguousarray(gid.reshape(86, 128).T)
    qwd_pad = np.zeros((8 * 176, 4096), np.uint32)
    qwd_pad[:1376] = qwd
    qzd_pad = np.zeros((96, 512), np.uint32)
    qzd_pad[:86] = qzd
    scd_pad = np.zeros((96, 4096), f32)
    scd_pad[:86] = scd
    gigT = gig.reshape(32, 128)
    gidT_pad = np.zeros((88, 128), f32)
    gidT_pad[:86] = gid.reshape(86, 128)

    nc = build_program()
    in_maps = []
    for c in range(8):
        m = dict(
            xT=np.ascontiguousarray(xT_full[:, c * T:(c + 1) * T]),
            qw_gu=np.ascontiguousarray(qwg[c * 64:(c + 1) * 64]),
            qz_gu=qzg, sc_gu=scg,
            gidx_gu=gidxg,
            gidx_guT=np.ascontiguousarray(gigT[c * 4:(c + 1) * 4]),
            qw_dn=np.ascontiguousarray(qwd_pad[c * 176:(c + 1) * 176]),
            qz_dn=qzd_pad, sc_dn=scd_pad,
            gidx_dn=gidxd,
            gidx_dnT=np.ascontiguousarray(gidT_pad[c * 11:(c + 1) * 11]),
        )
        for p in "qkvo":
            m[f"W{p}T"] = WT[p]
            m[f"b{p}T"] = bT[p]
        in_maps.append(m)
    res = run_bass_kernel_spmd(nc, in_maps, core_ids=list(range(8)))
    if DUMP:
        DBG_RESULTS.clear()
        for c in range(8):
            DBG_RESULTS.append({n: np.asarray(res.results[c]["dbg_" + n])
                                for n in build_program.dbg_names})
    parts = [np.asarray(res.results[c]["out"]) for c in range(8)]
    return np.concatenate(parts, axis=0).reshape(4, 2048, HID).astype(np.float32)
